# version 11
# baseline (speedup 1.0000x reference)
import contextlib
import numpy as np
import concourse.bass as bass
import concourse.mybir as mybir
from concourse.bass_utils import run_bass_kernel_spmd

F32 = mybir.dt.float32
F32R = mybir.dt.float32r
BF16 = mybir.dt.bfloat16
AF = mybir.ActivationFunctionType
OP = mybir.AluOpType
AX = mybir.AxisListType

DM = 4096
L = 2048
NB = 32
NST = 16
TC = 256
NCH = L // TC
DK = 32

ENGS = ("sync", "tensor", "vector", "scalar", "gpsimd")
SAFE_DIST = 2


class Sched:

    def __init__(self):
        self.plan = {e: [] for e in ENGS}
        self.cnt = {e: 0 for e in ENGS}
        self.qcnt = {}
        self.last_wait = {e: {} for e in ENGS}
        self.sems = {}
        self.used = {k: set() for k in ENGS}

    def _wait(self, eng, sem, val):
        if val <= 0:
            return
        if sem == eng and (self.cnt[eng] - val) > SAFE_DIST:
            return
        lw = self.last_wait[eng].get(sem, 0)
        if val <= lw:
            return
        self.last_wait[eng][sem] = val
        if sem in self.used:
            self.used[sem].add(val)
        self.plan[eng].append(("wait", sem, val))

    def op(self, eng, fn, deps=()):
        for (s, v) in deps:
            self._wait(eng, s, v)
        self.cnt[eng] += 1
        self.plan[eng].append(("op", fn, eng, self.cnt[eng]))
        return (eng, self.cnt[eng])

    def dma(self, q, sem, fn, deps=()):
        eng = "sync" if q == "qs" else "gpsimd"
        for (s, v) in deps:
            self._wait(eng, s, v)
        self.qcnt[sem] = self.qcnt.get(sem, 0) + 1
        self.plan[eng].append(("dma", fn, sem))
        return (sem, self.qcnt[sem] * 16)

    def raw(self, eng, fn):
        self.plan[eng].append(("raw", fn))

    def finalize(self):
        self.rank = {}
        for k, s in self.used.items():
            self.rank[k] = {v: i + 1 for i, v in enumerate(sorted(s))}
        fin = [(sem, n * 16) for sem, n in self.qcnt.items()]

        def emitter(eng):
            def run(h):
                for rec in self.plan[eng]:
                    if rec[0] == "wait":
                        _, sem, v = rec
                        if sem in self.rank:
                            v = self.rank[sem][v]
                        h.wait_ge(self.sems[sem], v)
                    elif rec[0] == "dma":
                        _, fn, sem = rec
                        fn(h).then_inc(self.sems[sem], 16)
                    elif rec[0] == "op":
                        _, fn, sem, idx = rec
                        inst = fn(h)
                        if idx in self.used[sem]:
                            inst.then_inc(self.sems[sem], 1)
                if eng == "sync":
                    for (sem, v) in fin:
                        h.wait_ge(self.sems[sem], v)
            return run
        return emitter


class Slot:
    def __init__(self):
        self.wr = None
        self.rd = {}

    def write(self, tok):
        self.wr = tok
        self.rd = {}

    def read(self, tok):
        self.rd[tok[0]] = tok

    def rdeps(self):
        return [self.wr] if self.wr else []

    def wdeps(self):
        d = list(self.rd.values())
        if self.wr:
            d.append(self.wr)
        return d


def TOK(*deps):
    out = []
    for d in deps:
        if d is None:
            continue
        if isinstance(d, list):
            out.extend(x for x in d if x)
        elif d:
            out.append(d)
    return out


_DEBUG_XMID = False
_N_CHUNKS = NCH
_SKIP_B = False
_B_LIMIT = 4
_B0_LIMIT = 99
_M_LIMIT = 4


def build_nc():
    nc = bass.Bass("TRN2", target_bir_lowering=False)
    d_x = nc.dram_tensor("x", [NCH, 128, NB, TC], F32R, kind="ExternalInput").ap()
    d_xw1 = nc.dram_tensor("xw1", [8, 128, 4, 128], F32R, kind="ExternalInput").ap()
    d_xw2 = nc.dram_tensor("xw2", [128, NB * 32], F32R, kind="ExternalInput").ap()
    d_dtw = nc.dram_tensor("dtw", [8, 128, 4, 128], F32R, kind="ExternalInput").ap()
    d_alog = nc.dram_tensor("alog", [128, NB * 16], F32, kind="ExternalInput").ap()
    d_dsb = nc.dram_tensor("dsb", [128, NB], F32, kind="ExternalInput").ap()
    d_dtb = nc.dram_tensor("dtb", [128, NB], F32, kind="ExternalInput").ap()
    d_ones = nc.dram_tensor("onesr", [65, 128], F32R, kind="ExternalInput").ap()
    d_onesb = nc.dram_tensor("onesb", [33, 128], BF16, kind="ExternalInput").ap()
    d_onesc = nc.dram_tensor("onesc", [128, 1], F32R, kind="ExternalInput").ap()
    d_onesbc = nc.dram_tensor("onesbc", [128, 1], BF16, kind="ExternalInput").ap()
    d_ident = nc.dram_tensor("ident", [128, 128], F32R, kind="ExternalInput").ap()
    d_identb = nc.dram_tensor("identb", [128, 128], BF16, kind="ExternalInput").ap()
    d_seg = nc.dram_tensor("seg", [128, 8], BF16, kind="ExternalInput").ap()
    d_rep = nc.dram_tensor("rep", [8, 128], F32R, kind="ExternalInput").ap()
    d_bcol = nc.dram_tensor("bcol", [128, NB], F32, kind="ExternalInput").ap()
    d_cst = nc.dram_tensor("cst", [128, 3], F32, kind="ExternalInput").ap()
    d_wq = nc.dram_tensor("wq", [128, NB * 256], F32, kind="ExternalInput").ap()
    d_wk = nc.dram_tensor("wk", [128, NB * 256], F32, kind="ExternalInput").ap()
    d_aff = nc.dram_tensor("aff", [128, NB * 128], F32, kind="ExternalInput").ap()
    d_bm = nc.dram_tensor("bm", [128, NB * 128], F32, kind="ExternalInput").ap()
    d_wt = nc.dram_tensor("wt", [NB, 128, NB, 128], F32, kind="ExternalInput").ap()
    d_xmid = nc.dram_tensor("xmid", [NCH, 128, NB, TC], F32R,
                            kind="ExternalOutput" if _DEBUG_XMID else "Internal").ap()
    d_out = nc.dram_tensor("out", [NB, 2, 4, 128, TC], F32, kind="ExternalOutput").ap()

    es = contextlib.ExitStack()
    sb = lambda name, shape, dt: es.enter_context(nc.sbuf_tensor(name, shape, dt))

    xbuf = sb("xbuf", [128, 2 * NB * TC], F32R)
    xln = sb("xln", [128, NB * TC], F32R)
    dtb16 = sb("dtb16", [128, NB * TC], BF16)
    hblk = sb("hblk", [128, NST * TC], BF16)
    s_xw1 = sb("s_xw1", [128, 1024], F32R)
    s_dtw = sb("s_dtw", [128, 1024], F32R)
    wk2 = sb("wk2", [128, 2048], F32R)
    wk3 = sb("wk3", [128, 2048], F32R)
    pool = sb("pool", [128, 2560], F32)
    prods = sb("prods", [128, 512], F32R)
    wqx = sb("wqx", [128, 4096], BF16)
    bbc = sb("bbc", [128, NST * TC], BF16)
    cbc = sb("cbc", [128, NST * TC], BF16)
    xp2f = sb("xp2f", [33, 16 * TC], BF16)
    s_xw2 = sb("s_xw2", [128, NB * 32], F32R)
    s_alog = sb("s_alog", [128, NB * 16], F32)
    hstate = sb("hstate", [128, NB * NST], F32)
    xp1s = sb("xp1s", [128, TC], F32R)
    xp2s = sb("xp2s", [32, TC], BF16)
    sqst = sb("sqst", [128, 2 * TC], BF16)
    s_D = sb("s_D", [128, NB], F32)
    s_dtb = sb("s_dtb", [128, NB], F32)
    s_bcol = sb("s_bcol", [128, NB], F32)
    s_cst = sb("s_cst", [128, 3], F32)
    s_ones = sb("s_ones", [65, 128], F32R)
    s_onesb = sb("s_onesb", [33, 128], BF16)
    s_onesc = sb("s_onesc", [128, 1], F32R)
    s_onesbc = sb("s_onesbc", [128, 1], BF16)
    s_ident = sb("s_ident", [128, 128], F32R)
    s_identb = sb("s_identb", [128, 128], BF16)
    s_seg = sb("s_seg", [128, 8], BF16)
    s_rep = sb("s_rep", [8, 128], F32R)
    rows_t = sb("rows_t", [1, 2 * TC], F32)
    rows_r = sb("rows_r", [1, 2 * TC], F32)
    scw = sb("scw", [68, TC], F32)
    wtst = sb("wtst", [8, TC], F32R)
    r_mx = sb("r_mx", [128, 1], F32)
    r_bias = sb("r_bias", [128, 1], F32)
    r_ex = sb("r_ex", [128, 8], F32)
    r_sum = sb("r_sum", [128, 1], F32)
    r_rec = sb("r_rec", [128, 1], F32)
    r_wsm = sb("r_wsm", [128, 8], F32)
    r_mean = rows_t[0:1, 0:TC]
    r_var = rows_t[0:1, TC:2 * TC]
    r_rstd = rows_r[0:1, 0:TC].bitcast(F32R)
    r_lnv = rows_r[0:1, TC:2 * TC].bitcast(F32R)
    r_mr = rows_r[0:1, TC:2 * TC].bitcast(F32R)
    sc1s = scw[0:4, :]
    sc2s = scw[64:68, :]
    wts = wtst[:]

    def xv_of(par):
        return xbuf[:, par * NB * TC:(par + 1) * NB * TC].rearrange(
            "p (b t) -> p b t", b=NB)
    xlnv = xln[:].rearrange("p (b t) -> p b t", b=NB)
    xw1v = s_xw1[:].rearrange("p (x b t) -> p x b t", x=2, b=4)
    dtwv = s_dtw[:].rearrange("p (x b t) -> p x b t", x=2, b=4)
    hblk_last = hblk[:].rearrange("p (n t) -> p n t", n=NST)[:, :, TC - 1]
    NPAR = 3
    a_scr = [pool[:, i * 256:(i + 1) * 256] for i in range(NPAR)]
    b_grp = [pool[:, 768:1792]] * 2
    prodb = prods[:].bitcast(BF16)
    e1_scr = [pool[:, 1792:2048]] * 2
    sqs = sqst[:]
    w_scr = [pool[:, 2048:2304]] * 2
    t1_scr = [pool[:, 2304:2560]] * 2
    res_v = [pool[:, i * 256:(i + 1) * 256] for i in range(2)]
    t1B = [pool[:, i * 256:(i + 1) * 256] for i in range(2)]
    outs_v = [pool[:, 512 + i * 256:512 + (i + 1) * 256] for i in range(2)]
    q_v = [pool[:, 1024 + i * 256:1024 + (i + 1) * 256] for i in range(2)]
    qk_v = [pool[:, 1536 + i * 128:1536 + (i + 1) * 128].bitcast(BF16)
            for i in range(2)]
    h_v = pool[:, 1792:2048]
    wh_v = pool[:, 2048:2560].bitcast(BF16)
    xln2h = xbuf[:].bitcast(BF16).rearrange("p (b t) -> p b t", b=NB)
    xmid_ch = xlnv
    wsl = dtb16[:].rearrange("p (w b o) -> p w b o", w=2, b=NB)
    wq_lo = hblk[:].rearrange("p (b t) -> p b t", b=16)
    wq_hi = wqx[:].rearrange("p (b t) -> p b t", b=16)
    wkA = wk2[:].bitcast(BF16).rearrange("p (b t) -> p b t", b=16)
    wkB = wk3[:].bitcast(BF16).rearrange("p (b t) -> p b t", b=16)
    affv = bbc[:].rearrange("p (b t) -> p b t", b=NB)
    bmv = cbc[:].rearrange("p (b t) -> p b t", b=NB)

    pb = [es.enter_context(nc.psum_tensor(f"pb{i}", [128, 512], F32)) for i in range(8)]

    S = Sched()
    sl_x = [Slot(), Slot()]
    sl_xln = Slot()
    sl_dt = Slot()
    sl_bc = Slot()
    sl_sq = [Slot(), Slot()]
    sl_a = [Slot() for _ in range(4)]
    sl_b = [Slot() for _ in range(4)]
    sl_prod = [Slot() for _ in range(4)]
    sl_pb = [Slot() for _ in range(8)]
    sl_xp1 = Slot()
    sl_xp2f = Slot()
    sl_rows = Slot()
    sl_wscr = [Slot(), Slot()]
    sl_e1 = [Slot(), Slot()]
    sl_t1 = [Slot(), Slot()]
    sl_xw1 = [Slot(), Slot()]
    sl_dtw = [Slot(), Slot()]

    t_w = {}
    for nm, dst, src in [
        ("xw2", s_xw2, d_xw2), ("alog", s_alog, d_alog), ("dsb", s_D, d_dsb),
        ("dtb", s_dtb, d_dtb), ("bcol", s_bcol, d_bcol),
        ("ones", s_ones, d_ones), ("onesb", s_onesb, d_onesb),
        ("ident", s_ident, d_ident), ("seg", s_seg, d_seg), ("rep", s_rep, d_rep),
        ("onesc", s_onesc, d_onesc), ("onesbc", s_onesbc, d_onesbc),
        ("cst", s_cst, d_cst), ("identb", s_identb, d_identb),
    ]:
        t_w[nm] = S.dma("qs", "w0", lambda h, dst=dst, src=src: h.dma_start(out=dst[:], in_=src))
    t_w = {k: ("w0", S.qcnt["w0"] * 16) for k in t_w}
    for c in range(2):
        tok = S.dma("qs", f"xb{c % 2}", lambda h, c=c: h.dma_start(
            out=xv_of(c % 2), in_=d_x[c]))
        sl_x[c % 2].write(tok)
    for qq in range(2):
        tok = S.dma("qs", f"xw1{qq}", lambda h, qq=qq: h.dma_start(out=xw1v[:, qq], in_=d_xw1[qq]))
        sl_xw1[qq].write(tok)
        tok = S.dma("qs", f"dtw{qq}", lambda h, qq=qq: h.dma_start(out=dtwv[:, qq], in_=d_dtw[qq]))
        sl_dtw[qq].write(tok)
    ta = S.op("scalar", lambda h: h.activation(pool[:, 0:512], s_alog[:], AF.Exp,
                                               scale=1.0, bias=0.0),
              deps=[t_w["alog"]])
    t_A = S.op("vector", lambda h: h.tensor_scalar_mul(s_alog[:], pool[:, 0:512], -1.0),
               deps=[ta])
    sl_a[0].write(t_A)
    sl_a[1].write(t_A)
    t_hs0 = S.op("vector", lambda h: h.memset(hstate[:], 0.0))

    stats_tok = {}

    def emit_stats(c):
        par = c % 2
        xv = xv_of(par)
        for k in range(NB):
            tsq = S.op("scalar", lambda h, k=k, xv=xv: h.activation(
                sqs[:, (k % 2) * TC:(k % 2 + 1) * TC], xv[:, k].bitcast(F32),
                AF.Square, scale=1.0, bias=0.0),
                deps=TOK(sl_x[par].rdeps(), sl_sq[k % 2].wdeps()))
            sl_sq[k % 2].write(tsq)
            sl_x[par].read(tsq)
            tmm = S.op("tensor", lambda h, k=k, xv=xv: h.matmul(
                pb[0][0:1, 0:TC], s_onesc[:].bitcast(F32), xv[:, k].bitcast(F32),
                start=(k == 0), stop=(k == NB - 1)),
                deps=TOK([t_w["onesc"]], sl_x[par].rdeps(),
                         sl_pb[0].wdeps() if k == 0 else []))
            sl_x[par].read(tmm)
            if k == 0:
                sl_pb[0].write(tmm)
            tm2 = S.op("tensor", lambda h, k=k: h.matmul(
                pb[1][0:1, 0:TC], s_onesbc[:], sqs[:, (k % 2) * TC:(k % 2 + 1) * TC],
                start=(k == 0), stop=(k == NB - 1)),
                deps=TOK([t_w["onesbc"], tsq], sl_pb[1].wdeps() if k == 0 else []))
            sl_sq[k % 2].read(tm2)
            if k == 0:
                sl_pb[1].write(tm2)
            t_sx_done, t_sxx_done = tmm, tm2
        td = S.op("vector", lambda h: h.tensor_scalar_mul(
            r_mean, pb[0][0:1, 0:TC], 1.0 / DM),
            deps=TOK([t_sx_done], sl_rows.wdeps()))
        sl_rows.write(td)
        sl_pb[0].read(td)
        td2 = S.op("vector", lambda h: h.tensor_mul(r_var, r_mean, r_mean),
                   deps=[td])
        td3 = S.op("vector", lambda h: h.scalar_tensor_tensor(
            r_var, pb[1][0:1, 0:TC], 1.0 / DM, r_var, OP.mult, OP.subtract),
            deps=TOK([t_sxx_done, td2]))
        sl_pb[1].read(td3)
        td4 = S.op("scalar", lambda h: h.activation(r_lnv, r_var, AF.Ln,
                                                    scale=1.0, bias=s_cst[0:1, 0:1]), deps=TOK([td3, t_w['cst']]))
        td5 = S.op("scalar", lambda h: h.activation(r_rstd, r_lnv.bitcast(F32), AF.Exp,
                                                    scale=s_cst[0:1, 1:2], bias=0.0), deps=[td4])
        td6 = S.op("vector", lambda h: h.tensor_mul(r_mr, r_mean,
                                                    r_rstd.bitcast(F32)), deps=[td5])
        for _t in (td4, td5, td6):
            sl_rows.read(_t)
        tb1 = S.op("tensor", lambda h: h.matmul(pb[0][:, 0:TC], s_ones[0:1, :], r_rstd,
                                                start=True, stop=True),
                   deps=TOK([td5], sl_pb[0].wdeps()))
        sl_pb[0].write(tb1)
        sl_rows.read(tb1)
        tb2 = S.op("tensor", lambda h: h.matmul(pb[1][:, 0:TC], s_ones[0:1, :], r_mr,
                                                start=True, stop=True),
                   deps=TOK([td6], sl_pb[1].wdeps()))
        sl_pb[1].write(tb2)
        sl_rows.read(tb2)
        stats_tok[c] = (tb1, tb2)

    for c in range(_N_CHUNKS):
        par = c % 2
        xv = xv_of(par)
        if c == 0:
            emit_stats(0)
        tb1, tb2 = stats_tok[c]
        t_xln = []
        for k in range(NB):
            tt1 = S.op("vector", lambda h, k=k, xv=xv: h.tensor_mul(
                t1_scr[k % 2], xv[:, k].bitcast(F32), pb[0][:, 0:TC]),
                deps=TOK([tb1], sl_t1[k % 2].wdeps(), sl_x[par].rdeps()))
            sl_x[par].read(tt1)
            sl_pb[0].read(tt1)
            tt2 = S.op("vector", lambda h, k=k: h.tensor_tensor(
                xlnv[:, k], t1_scr[k % 2], pb[1][:, 0:TC], OP.subtract),
                deps=TOK([tb2], sl_xln.wdeps() if k == 0 else []))
            sl_t1[k % 2].write(tt2)
            sl_pb[1].read(tt2)
            if k == 0:
                sl_xln.write(tt2)
            t_xln.append(tt2)
            tx2 = S.op("vector", lambda h, k=k, xv=xv: h.scalar_tensor_tensor(
                xv[:, k], xlnv[:, k].bitcast(F32), s_D[:, k:k + 1],
                xv[:, k].bitcast(F32), OP.mult, OP.add),
                deps=TOK([t_w["dsb"], tt2]))
            sl_x[par].read(tx2)
            sl_xln.read(tx2)
        for k in range(NB):
            qq = k // 4
            t = S.op("tensor", lambda h, k=k, qq=qq: h.matmul(
                pb[2][:, 0:TC], xw1v[:, qq % 2, k % 4], xlnv[:, k],
                start=(k == 0), stop=(k == NB - 1)),
                deps=TOK(sl_xw1[qq % 2].rdeps(), [t_xln[k]],
                         sl_pb[2].wdeps() if k == 0 else []))
            sl_xln.read(t)
            sl_xw1[qq % 2].read(t)
            if k == 0:
                sl_pb[2].write(t)
            t_xp1 = t
            if k % 4 == 3 and (c < NCH - 1 or qq < 7):
                nq = (qq + 1) % 8
                tok = S.dma("qs", f"xw1{nq % 2}", lambda h, nq=nq: h.dma_start(
                    out=xw1v[:, nq % 2], in_=d_xw1[nq]),
                    deps=TOK(sl_xw1[nq % 2].wdeps()))
                sl_xw1[nq % 2].write(tok)
        for k in range(NB):
            t = S.op("tensor", lambda h, k=k: h.matmul(
                pb[3][0:32, 0:TC], s_xw2[:, k * 32:(k + 1) * 32], xlnv[:, k],
                start=(k == 0), stop=(k == NB - 1)),
                deps=TOK([t_w["xw2"], t_xln[k]], sl_pb[3].wdeps() if k == 0 else []))
            sl_xln.read(t)
            if k == 0:
                sl_pb[3].write(t)
            t_xp2 = t
        tev1 = S.op("scalar", lambda h: h.activation(xp1s[:], pb[2][:, 0:TC], AF.Copy,
                                                     scale=1.0, bias=0.0),
                    deps=TOK([t_xp1], sl_xp1.wdeps()))
        sl_pb[2].read(tev1)
        sl_xp1.write(tev1)
        tev2 = S.op("scalar", lambda h: h.activation(xp2s[:], pb[3][0:32, 0:TC], AF.Copy,
                                                     scale=1.0, bias=0.0),
                    deps=TOK([t_xp2], sl_xp2f.wdeps()))
        sl_pb[3].read(tev2)
        tfl = S.dma("qg", "xpf", lambda h: h.dma_start(out=xp2f[0:1, :], in_=xp2s[0:16, :]),
                    deps=TOK([tev2]))
        tfl = S.dma("qg", "xpf", lambda h: h.dma_start(out=xp2f[32:33, :], in_=xp2s[16:32, :]),
                    deps=TOK([tev2]))
        sl_xp2f.write(tfl)
        t_dt = []
        for k in range(NB):
            qq = k // 4
            tz = S.op("tensor", lambda h, k=k, qq=qq: h.matmul(
                pb[4 + k % 2][:, 0:TC], dtwv[:, qq % 2, k % 4], xp1s[:],
                start=True, stop=True),
                deps=TOK(sl_dtw[qq % 2].rdeps(), [tev1], sl_pb[4 + k % 2].wdeps()))
            sl_pb[4 + k % 2].write(tz)
            sl_xp1.read(tz)
            sl_dtw[qq % 2].read(tz)
            te = S.op("scalar", lambda h, k=k: h.activation(
                e1_scr[k % 2], pb[4 + k % 2][:, 0:TC], AF.Exp,
                scale=1.0, bias=s_dtb[:, k:k + 1]),
                deps=TOK([t_w["dtb"], tz], sl_e1[k % 2].wdeps()))
            sl_pb[4 + k % 2].read(te)
            sl_e1[k % 2].write(te)
            tl = S.op("scalar", lambda h, k=k: h.activation(
                dtb16[:, k * TC:(k + 1) * TC], e1_scr[k % 2], AF.Ln,
                scale=1.0, bias=1.0),
                deps=TOK([te], sl_dt.wdeps() if k == 0 else []))
            sl_e1[k % 2].read(tl)
            if k == 0:
                sl_dt.write(tl)
            t_dt.append(tl)
            if k % 4 == 3 and (c < NCH - 1 or qq < 7):
                nq = (qq + 1) % 8
                tok = S.dma("qs", f"dtw{nq % 2}", lambda h, nq=nq: h.dma_start(
                    out=dtwv[:, nq % 2], in_=d_dtw[nq]),
                    deps=TOK(sl_dtw[nq % 2].wdeps()))
                sl_dtw[nq % 2].write(tok)
        t_bcB = [None] * 8
        t_bcC = [None] * 8
        for j in range(16):
            isB = j < 8
            tgt = bbc if isB else cbc
            jn = j % 8
            row = xp2f[0:1, :] if isB else xp2f[32:33, :]
            lhs1 = s_onesb[0:1, :] if isB else s_onesb[32:33, :]
            tb = S.op("tensor", lambda h, jn=jn, row=row, lhs1=lhs1, j=j: h.matmul(
                pb[2 + j % 2][:, 0:2 * TC], lhs1, row[:, jn * 2 * TC:(jn + 1) * 2 * TC],
                start=True, stop=True),
                deps=TOK([tfl, t_w["onesb"]], sl_pb[2 + j % 2].wdeps()))
            sl_pb[2 + j % 2].write(tb)
            sl_xp2f.read(tb)
            tv = S.op("scalar", lambda h, tgt=tgt, jn=jn, j=j: h.activation(
                tgt[:, jn * 2 * TC:(jn + 1) * 2 * TC], pb[2 + j % 2][:, 0:2 * TC],
                AF.Copy, scale=1.0, bias=0.0),
                deps=TOK([tb], sl_bc.wdeps() if j == 0 else []))
            sl_pb[2 + j % 2].read(tv)
            if j == 0:
                sl_bc.write(tv)
            if isB:
                t_bcB[jn] = tv
            else:
                t_bcC[jn] = tv
        t_bcast = t_bcC[7]
        for k in range(NB):
            kp = k % 2
            tw = S.op("vector", lambda h, k=k, kp=kp: h.tensor_tensor(
                w_scr[kp], dtb16[:, k * TC:(k + 1) * TC], xlnv[:, k].bitcast(F32),
                OP.mult),
                deps=TOK([t_dt[k]], sl_wscr[kp].wdeps()))
            sl_dt.read(tw)
            sl_xln.read(tw)
            sl_wscr[kp].write(tw)
            t_aexp = [None] * NST
            t_bb = [None] * 4
            t_scan = [None] * NST
            t_cm = [None] * 4
            t_id = [None] * NST

            def emit_a(n, k=k, t_aexp=t_aexp):
                p4 = n % NPAR
                t = S.op("scalar", lambda h, k=k, n=n, p4=p4: h.activation(
                    a_scr[p4], dtb16[:, k * TC:(k + 1) * TC], AF.Exp,
                    scale=s_alog[:, k * NST + n:k * NST + n + 1], bias=0.0),
                    deps=TOK([t_A, t_dt[k]], sl_a[p4].wdeps()))
                sl_a[p4].write(t)
                sl_dt.read(t)
                t_aexp[n] = t

            def emit_bgrp(g, k=k, kp=kp, t_bb=t_bb, tw=tw):
                gp = g % 2
                t = S.op("vector", lambda h, g=g, gp=gp, kp=kp: h.tensor_tensor(
                    b_grp[gp].rearrange("p (x t) -> p x t", x=4),
                    w_scr[kp].unsqueeze(1).broadcast_to((128, 4, TC)),
                    bbc[:, g * 4 * TC:(g + 1) * 4 * TC].rearrange(
                        "p (x t) -> p x t", x=4), OP.mult),
                    deps=TOK([tw, t_bcB[g * 2 + 1]], sl_b[gp].wdeps()))
                sl_wscr[kp].read(t)
                sl_bc.read(t)
                sl_b[gp].write(t)
                t_bb[g] = t

            def emit_scan(n, k=k, t_scan=t_scan, t_aexp=t_aexp, t_bb=t_bb):
                p4 = n % NPAR
                g, gp = n // 4, (n // 4) % 2
                t = S.op("vector", lambda h, k=k, n=n, p4=p4, gp=gp: h.tensor_tensor_scan(
                    hblk[:, n * TC:(n + 1) * TC], a_scr[p4],
                    b_grp[gp][:, (n % 4) * TC:(n % 4 + 1) * TC],
                    hstate[:, k * NST + n:k * NST + n + 1], OP.mult, OP.add),
                    deps=TOK([t_aexp[n], t_bb[g], t_hs0]))
                sl_a[p4].read(t)
                sl_b[gp].read(t)
                t_scan[n] = t

            def emit_cmgrp(g, t_cm=t_cm, t_scan=t_scan):
                t = S.op("vector", lambda h, g=g: h.tensor_tensor(
                    prodb, hblk[:, g * 4 * TC:(g + 1) * 4 * TC],
                    cbc[:, g * 4 * TC:(g + 1) * 4 * TC], OP.mult),
                    deps=TOK([t_scan[g * 4 + 3], t_bcC[g * 2 + 1]], sl_prod[0].wdeps()))
                sl_prod[0].write(t)
                sl_bc.read(t)
                t_cm[g] = t

            def emit_id(n, k=k, t_id=t_id, t_cm=t_cm):
                t = S.op("tensor", lambda h, k=k, n=n: h.matmul(
                    pb[6 + k % 2][:, 0:TC], s_identb[:],
                    prodb[:, (n % 4) * TC:(n % 4 + 1) * TC],
                    start=(n == 0), stop=(n == NST - 1)),
                    deps=TOK([t_w["identb"], t_cm[n // 4]],
                             sl_pb[6 + k % 2].wdeps() if n == 0 else []))
                sl_prod[0].read(t)
                if n == 0:
                    sl_pb[6 + k % 2].write(t)
                t_id[n] = t

            emit_a(0)
            emit_bgrp(0)
            emit_a(1)
            for n in range(NST):
                emit_scan(n)
                if n + 2 < NST:
                    emit_a(n + 2)
                if n % 4 == 3 and n // 4 + 1 < 4:
                    emit_bgrp(n // 4 + 1)
                if n % 4 == 0 and n >= 4:
                    g = n // 4 - 1
                    emit_cmgrp(g)
                    for j in range(4):
                        emit_id(g * 4 + j)
            emit_cmgrp(3)
            for j in range(4):
                emit_id(12 + j)
            S.op("vector", lambda h, k=k: h.tensor_copy(
                hstate[:, k * NST:(k + 1) * NST], hblk_last),
                deps=TOK([t_scan[NST - 1]]))
            ty = S.op("vector", lambda h, k=k, xv=xv: h.tensor_tensor(
                xv[:, k], xv[:, k].bitcast(F32),
                pb[6 + k % 2][:, 0:TC], OP.add),
                deps=TOK([t_id[NST - 1]]))
            sl_pb[6 + k % 2].read(ty)
            sl_x[par].read(ty)
            if k == 20 and c + 1 < NCH:
                emit_stats(c + 1)
        tst = S.dma("qg", f"xms{par}", lambda h, c=c, xv=xv: h.dma_start(out=d_xmid[c], in_=xv),
                    deps=TOK([sl_x[par].rd.get("vector")]))
        sl_x[par].read(tst)
        if c + 2 < NCH:
            tok = S.dma("qs", f"xb{par}", lambda h, c=c: h.dma_start(
                out=xv_of(c % 2), in_=d_x[c + 2]), deps=TOK(sl_x[par].wdeps()))
            sl_x[par].write(tok)

    allbar = [(e, S.cnt[e]) for e in ENGS if S.cnt[e] > 0]
    allbar += [(sem, n * 16) for sem, n in S.qcnt.items()]

    t_wq = S.dma("qg", "wB", lambda h: h.dma_start(out=hblk[:], in_=d_wq[:, 0:16 * 256]),
                 deps=allbar)
    t_wq2 = S.dma("qg", "wB", lambda h: h.dma_start(out=wqx[:], in_=d_wq[:, 16 * 256:NB * 256]),
                  deps=allbar)
    t_wkA = S.dma("qg", "wB", lambda h: h.dma_start(out=wk2[:].bitcast(BF16),
                                                    in_=d_wk[:, 0:16 * 256]), deps=allbar)
    t_wkB = S.dma("qg", "wB", lambda h: h.dma_start(out=wk3[:].bitcast(BF16),
                                                    in_=d_wk[:, 16 * 256:32 * 256]),
                  deps=allbar)
    t_aff = S.dma("qg", "wB", lambda h: h.dma_start(out=bbc[:], in_=d_aff),
                  deps=allbar)
    t_bm = S.dma("qg", "wB", lambda h: h.dma_start(out=cbc[:], in_=d_bm),
                 deps=allbar)
    t_wq = t_wkA = t_wkB = t_aff = t_bm = ("wB", S.qcnt["wB"] * 16)
    sl_xl2 = Slot()
    sl_wh = Slot()
    sl_wsl = [Slot(), Slot()]
    sl_res = [Slot(), Slot()]
    sl_outs = [Slot(), Slot()]
    sl_xmc = Slot()
    sl_qv = [Slot(), Slot()]
    sl_qkv = [Slot(), Slot()]
    sl_hv = Slot()
    sl_sc = Slot()
    sl_wts = Slot()

    prev_soft = {"tex": None, "twm": None, "ttb": None}
    for half in range(0 if _SKIP_B else (2 if _B_LIMIT >= 4 else 1)):
        for cc in range(4 if _B_LIMIT >= 2 else 0):
            g = half * 4 + cc
            tld = S.dma("qs", "xmc", lambda h, g=g: h.dma_start(out=xmid_ch, in_=d_xmid[g]),
                        deps=TOK(allbar if (half == 0 and cc == 0) else [],
                                 sl_xmc.wdeps()))
            sl_xmc.write(tld)
            for k in range(NB):
                tsq = S.op("scalar", lambda h, k=k: h.activation(
                    sqs[:, (k % 2) * TC:(k % 2 + 1) * TC], xmid_ch[:, k].bitcast(F32),
                    AF.Square, scale=1.0, bias=0.0),
                    deps=TOK([tld], sl_sq[k % 2].wdeps(),
                             allbar if (half == 0 and cc == 0 and k == 0) else []))
                sl_sq[k % 2].write(tsq)
                sl_xmc.read(tsq)
                tmm = S.op("tensor", lambda h, k=k: h.matmul(
                    pb[0][0:1, 0:TC], s_onesc[:], xmid_ch[:, k],
                    start=(k == 0), stop=(k == NB - 1)),
                    deps=TOK([tld], sl_pb[0].wdeps() if k == 0 else [],
                             allbar if (half == 0 and cc == 0 and k == 0) else []))
                sl_xmc.read(tmm)
                if k == 0:
                    sl_pb[0].write(tmm)
                tm2 = S.op("tensor", lambda h, k=k: h.matmul(
                    pb[1][0:1, 0:TC], s_onesbc[:], sqs[:, (k % 2) * TC:(k % 2 + 1) * TC],
                    start=(k == 0), stop=(k == NB - 1)),
                    deps=TOK([tsq], sl_pb[1].wdeps() if k == 0 else []))
                sl_sq[k % 2].read(tm2)
                if k == 0:
                    sl_pb[1].write(tm2)
                t_sx_done, t_sxx_done = tmm, tm2
            td = S.op("vector", lambda h: h.tensor_scalar_mul(
                r_mean, pb[0][0:1, 0:TC], 1.0 / DM),
                deps=TOK([t_sx_done], sl_rows.wdeps(),
                         allbar if (half == 0 and cc == 0) else []))
            sl_rows.write(td)
            sl_pb[0].read(td)
            td2 = S.op("vector", lambda h: h.tensor_mul(r_var, r_mean, r_mean),
                       deps=[td])
            td3 = S.op("vector", lambda h: h.scalar_tensor_tensor(
                r_var, pb[1][0:1, 0:TC], 1.0 / DM, r_var, OP.mult, OP.subtract),
                deps=TOK([t_sxx_done, td2]))
            sl_pb[1].read(td3)
            td4 = S.op("scalar", lambda h: h.activation(r_lnv, r_var, AF.Ln,
                                                        scale=1.0, bias=s_cst[0:1, 0:1]),
                       deps=TOK([td3, t_w['cst']]))
            td5 = S.op("scalar", lambda h: h.activation(r_rstd, r_lnv.bitcast(F32), AF.Exp,
                                                        scale=s_cst[0:1, 1:2], bias=0.0),
                       deps=[td4])
            td6 = S.op("vector", lambda h: h.tensor_mul(
                r_mr, r_mean, r_rstd.bitcast(F32)), deps=[td5])
            for _t in (td4, td5, td6):
                sl_rows.read(_t)
            tb1 = S.op("tensor", lambda h: h.matmul(pb[0][:, 0:TC], s_ones[0:1, :],
                                                    r_rstd, start=True, stop=True),
                       deps=TOK([td5], sl_pb[0].wdeps()))
            sl_pb[0].write(tb1)
            sl_rows.read(tb1)
            tb2 = S.op("tensor", lambda h: h.matmul(pb[1][:, 0:TC], s_ones[0:1, :], r_mr,
                                                    start=True, stop=True),
                       deps=TOK([td6], sl_pb[1].wdeps()))
            sl_pb[1].write(tb2)
            sl_rows.read(tb2)
            if _B0_LIMIT < 2:
                continue
            t_x2 = []
            for k in range(NB):
                tt1 = S.op("vector", lambda h, k=k: h.tensor_mul(
                    t1B[k % 2], xmid_ch[:, k].bitcast(F32), pb[0][:, 0:TC]),
                    deps=TOK([tb1], sl_t1[k % 2].wdeps(), sl_res[k % 2].wdeps()))
                sl_xmc.read(tt1)
                sl_pb[0].read(tt1)
                sl_res[k % 2].read(tt1)
                tt2 = S.op("vector", lambda h, k=k, cc=cc: h.tensor_tensor(
                    xln2h[:, k, cc * TC:(cc + 1) * TC], t1B[k % 2],
                    pb[1][:, 0:TC], OP.subtract),
                    deps=TOK([tb2], sl_xl2.wdeps() if (k == 0 and cc == 0) else []))
                sl_t1[k % 2].write(tt2)
                sl_res[k % 2].read(tt2)
                sl_pb[1].read(tt2)
                if k == 0 and cc == 0:
                    sl_xl2.write(tt2)
                t_x2.append(tt2)
            if _B0_LIMIT < 3:
                continue
            for k in range(NB):
                wq_src = wq_lo if k < 16 else wq_hi
                tq1 = S.op("tensor", lambda h, k=k, cc=cc, wq_src=wq_src: h.matmul(
                    pb[2][:, 0:TC], wq_src[:, k % 16, 0:128],
                    xln2h[:, k, cc * TC:(cc + 1) * TC],
                    start=(k == 0), stop=(k == NB - 1)),
                    deps=TOK([t_wq, t_x2[k]], sl_pb[2].wdeps() if k == 0 else []))
                sl_xl2.read(tq1)
                if k == 0:
                    sl_pb[2].write(tq1)
                tq2 = S.op("tensor", lambda h, k=k, cc=cc, wq_src=wq_src: h.matmul(
                    pb[3][:, 0:TC], wq_src[:, k % 16, 128:256],
                    xln2h[:, k, cc * TC:(cc + 1) * TC],
                    start=(k == 0), stop=(k == NB - 1)),
                    deps=TOK([t_wq, t_x2[k]], sl_pb[3].wdeps() if k == 0 else []))
                if k == 0:
                    sl_pb[3].write(tq2)
                wk_src = wkA if k < 16 else wkB
                tk1 = S.op("tensor", lambda h, k=k, cc=cc, wk_src=wk_src: h.matmul(
                    pb[4][:, 0:TC], wk_src[:, k % 16, 0:128],
                    xln2h[:, k, cc * TC:(cc + 1) * TC],
                    start=(k == 0), stop=(k == NB - 1)),
                    deps=TOK([t_wkA, t_wkB, t_x2[k]],
                             sl_pb[4].wdeps() if k == 0 else []))
                if k == 0:
                    sl_pb[4].write(tk1)
                tk2 = S.op("tensor", lambda h, k=k, cc=cc, wk_src=wk_src: h.matmul(
                    pb[5][:, 0:TC], wk_src[:, k % 16, 128:256],
                    xln2h[:, k, cc * TC:(cc + 1) * TC],
                    start=(k == 0), stop=(k == NB - 1)),
                    deps=TOK([t_wkA, t_wkB, t_x2[k]],
                             sl_pb[5].wdeps() if k == 0 else []))
                if k == 0:
                    sl_pb[5].write(tk2)
                th = S.op("tensor", lambda h, k=k, cc=cc: h.matmul(
                    pb[6][:, 0:TC], affv[:, k], xln2h[:, k, cc * TC:(cc + 1) * TC],
                    start=(k == 0), stop=(k == NB - 1)),
                    deps=TOK([t_aff, t_x2[k]], sl_pb[6].wdeps() if k == 0 else []))
                if k == 0:
                    sl_pb[6].write(th)
                t_qd, t_q2d, t_kd, t_k2d, t_hd = tq1, tq2, tk1, tk2, th
            if _B0_LIMIT < 4:
                continue
            t_sc_ev = []
            for g2 in range(2):
                tevq = S.op("scalar", lambda h, g2=g2: h.activation(
                    q_v[g2], pb[2 + g2][:, 0:TC], AF.Copy, scale=1.0, bias=0.0),
                    deps=TOK([t_qd if g2 == 0 else t_q2d], sl_qv[g2].wdeps()))
                sl_pb[2 + g2].read(tevq)
                sl_qv[g2].write(tevq)
                tqk = S.op("vector", lambda h, g2=g2: h.tensor_tensor(
                    qk_v[g2], pb[4 + g2][:, 0:TC], q_v[g2], OP.mult),
                    deps=TOK([tevq, t_kd if g2 == 0 else t_k2d], sl_qkv[g2].wdeps()))
                sl_pb[4 + g2].read(tqk)
                sl_qv[g2].read(tqk)
                sl_qkv[g2].write(tqk)
                bank = pb[7] if g2 == 0 else pb[0]
                base = 0 if g2 == 0 else 64
                bslot = sl_pb[7] if g2 == 0 else sl_pb[0]
                tsc_mm = S.op("tensor", lambda h, g2=g2, base=base, bank=bank: h.matmul(
                    bank[base:base + 4, 0:TC], s_seg[:, g2 * 4:(g2 + 1) * 4],
                    qk_v[g2], start=True, stop=True),
                    deps=TOK([t_w["seg"], tqk], bslot.wdeps()))
                sl_qkv[g2].read(tsc_mm)
                bslot.write(tsc_mm)
                tevs = S.op("scalar", lambda h, g2=g2, base=base, bank=bank: h.activation(
                    (sc1s if g2 == 0 else sc2s)[:], bank[base:base + 4, 0:TC],
                    AF.Copy, scale=1.0, bias=0.0),
                    deps=TOK([tsc_mm], sl_sc.wdeps() if g2 == 0 else []))
                bslot.read(tevs)
                if g2 == 0:
                    sl_sc.write(tevs)
                t_sc_ev.append(tevs)
            if _B0_LIMIT < 5:
                continue
            for ts in range(2):
                ttr1 = S.op("tensor", lambda h, ts=ts: h.transpose(
                    pb[1][:, ts * 8:ts * 8 + 4], sc1s[:, ts * 128:(ts + 1) * 128],
                    s_ident[0:4, 0:4].bitcast(F32)),
                    deps=TOK(t_sc_ev, sl_pb[1].wdeps()))
                sl_pb[1].write(ttr1)
                sl_sc.read(ttr1)
                ttr2 = S.op("tensor", lambda h, ts=ts: h.transpose(
                    pb[1][:, ts * 8 + 4:ts * 8 + 8], sc2s[:, ts * 128:(ts + 1) * 128],
                    s_ident[64:68, 64:68].bitcast(F32)),
                    deps=TOK(t_sc_ev))
                sl_sc.read(ttr2)
                if _B0_LIMIT < 6:
                    continue
                tmx = S.op("vector", lambda h, ts=ts: h.reduce_max(
                    r_mx[:], pb[1][:, ts * 8:ts * 8 + 8], axis=AX.X),
                    deps=TOK([ttr2, prev_soft["tex"]]))
                sl_pb[1].read(tmx)
                tbs = S.op("vector", lambda h: h.tensor_scalar_mul(
                    r_bias[:], r_mx[:], -1.0 / float(np.sqrt(DK))),
                    deps=TOK([tmx, prev_soft["tex"]]))
                if _B0_LIMIT < 7:
                    continue
                tex = S.op("scalar", lambda h, ts=ts: h.activation(
                    r_ex[:], pb[1][:, ts * 8:ts * 8 + 8], AF.Exp,
                    scale=s_cst[:, 2:3], bias=r_bias[:]),
                    deps=TOK([tbs, prev_soft["twm"]]))
                sl_pb[1].read(tex)
                if _B0_LIMIT < 8:
                    continue
                tsm = S.op("vector", lambda h: h.reduce_sum(
                    r_sum[:], r_ex[:], axis=AX.X), deps=TOK([tex]))
                trc = S.op("vector", lambda h: h.reciprocal(r_rec[:], r_sum[:]),
                           deps=[tsm])
                twm = S.op("vector", lambda h: h.tensor_scalar(
                    r_wsm[:].bitcast(BF16)[:, 0:8], r_ex[:], r_rec[:], None, OP.mult),
                    deps=TOK([trc, tex, prev_soft["ttb"]]))
                if _B0_LIMIT < 9:
                    continue
                ttb = S.op("tensor", lambda h: h.transpose(
                    pb[0][:].bitcast(BF16)[0:8, 0:128], r_wsm[:].bitcast(BF16)[:, 0:8],
                    s_identb[:]),
                    deps=TOK([twm, t_w["identb"]], sl_pb[0].wdeps()))
                sl_pb[0].write(ttb)
                tevw = S.op("scalar", lambda h, ts=ts: h.activation(
                    wts[:, ts * 128:(ts + 1) * 128],
                    pb[0][:].bitcast(BF16)[0:8, 0:128], AF.Copy,
                    scale=1.0, bias=0.0),
                    deps=TOK([ttb], sl_wts.wdeps() if ts == 0 else []))
                sl_pb[0].read(tevw)
                if ts == 0:
                    sl_wts.write(tevw)
                prev_soft = {"tex": tex, "twm": twm, "ttb": ttb}
                t_wts = tevw
            if _B0_LIMIT < 10:
                continue
            twbc = S.op("tensor", lambda h: h.matmul(
                pb[0][:, 0:TC], s_rep[:], wts, start=True, stop=True),
                deps=TOK([t_w["rep"], t_wts], sl_pb[0].wdeps()))
            sl_pb[0].write(twbc)
            sl_wts.read(twbc)
            tevh = S.op("scalar", lambda h: h.activation(
                h_v, pb[6][:, 0:TC], AF.Copy, scale=1.0, bias=0.0),
                deps=TOK([t_hd], sl_hv.wdeps()))
            sl_pb[6].read(tevh)
            sl_hv.write(tevh)
            twh = S.op("vector", lambda h, cc=cc: h.tensor_tensor(
                wh_v[:, cc * TC:(cc + 1) * TC], h_v, pb[0][:, 0:TC], OP.mult),
                deps=TOK([tevh, twbc], sl_wh.wdeps() if cc == 0 else []))
            sl_pb[0].read(twh)
            sl_hv.read(twh)
            if cc == 0:
                sl_wh.write(twh)
            t_wh = twh
        t_wsl_tok = [None, None]
        for ob in range(2 if _B_LIMIT >= 3 else 0):
            t = S.dma("qg", f"wsl{ob % 2}", lambda h, ob=ob: h.dma_start(
                out=wsl[:, ob % 2], in_=d_wt[ob]),
                deps=TOK(sl_wsl[ob % 2].wdeps(),
                         allbar if (half == 0 and ob == 0) else []))
            sl_wsl[ob % 2].write(t)
            t_wsl_tok[ob % 2] = t
        for ob in range(NB if _B_LIMIT >= 3 else 0):
            pw = ob % 2
            t_res = [None] * 4
            for tcx in range(2):
                t_res[tcx] = S.dma("qs", f"res{tcx % 2}", lambda h, ob=ob, half=half, tcx=tcx: h.dma_start(
                    out=res_v[tcx % 2].bitcast(F32R),
                    in_=d_xmid[half * 4 + tcx, :, ob, :]),
                    deps=TOK(sl_res[tcx % 2].wdeps()))
                sl_res[tcx % 2].write(t_res[tcx])
            for tch in range(4 if _M_LIMIT >= 2 else 0):
                pk = tch % 2
                if tch >= 2:
                    t_res[tch] = S.dma("qs", f"res{tch % 2}", lambda h, ob=ob, half=half, tch=tch: h.dma_start(
                        out=res_v[tch % 2].bitcast(F32R),
                        in_=d_xmid[half * 4 + tch, :, ob, :]),
                        deps=TOK(sl_res[tch % 2].wdeps()))
                    sl_res[tch % 2].write(t_res[tch])
                t_last = None
                for k in range(NB):
                    t_last = S.op("tensor", lambda h, k=k, ob=ob, tch=tch, pk=pk: h.matmul(
                        pb[2 + pk][:, 0:TC], wsl[:, ob % 2, k],
                        xln2h[:, k, tch * TC:(tch + 1) * TC],
                        start=(k == 0), stop=False),
                        deps=TOK([t_wsl_tok[pw] if k == 0 else None,
                                  t_wh if k == 0 else None],
                                 sl_pb[2 + pk].wdeps() if k == 0 else []))
                    if k == 0:
                        sl_pb[2 + pk].write(t_last)
                    sl_wsl[ob % 2].read(t_last)
                    sl_xl2.read(t_last)
                if _M_LIMIT >= 3:
                    t_last = S.op("tensor", lambda h, ob=ob, tch=tch, pk=pk: h.matmul(
                        pb[2 + pk][:, 0:TC], bmv[:, ob],
                        wh_v[:, tch * TC:(tch + 1) * TC],
                        start=False, stop=True),
                        deps=TOK([t_bm, t_wh]))
                    sl_wh.read(t_last)
                else:
                    t_last = S.op("tensor", lambda h, ob=ob, tch=tch, pk=pk: h.matmul(
                        pb[2 + pk][:, 0:TC], s_ident[:],
                        prods[:, 0:TC], start=False, stop=True),
                        deps=TOK([t_w["ident"]]))
                if _M_LIMIT >= 4:
                    tev = S.op("vector", lambda h, ob=ob, tch=tch, pk=pk:
                               h.scalar_tensor_tensor(
                        outs_v[pk], pb[2 + pk][:, 0:TC], s_bcol[:, ob:ob + 1],
                        res_v[tch % 2].bitcast(F32R).bitcast(F32),
                        OP.add, OP.add),
                        deps=TOK([t_last, t_res[tch], t_w["bcol"]],
                                 sl_outs[pk].wdeps()))
                else:
                    tev = S.op("vector", lambda h, ob=ob, tch=tch, pk=pk:
                               h.tensor_copy(outs_v[pk], pb[2 + pk][:, 0:TC]),
                        deps=TOK([t_last, t_res[tch]], sl_outs[pk].wdeps()))
                sl_pb[2 + pk].read(tev)
                sl_res[tch % 2].read(tev)
                sl_outs[pk].write(tev)
                tso = S.dma("qs", f"st{pk}", lambda h, ob=ob, half=half, tch=tch, pk=pk:
                            h.dma_start(out=d_out[ob, half, tch], in_=outs_v[pk]),
                            deps=TOK([tev]))
                sl_outs[pk].read(tso)
            if ob + 2 < NB:
                t = S.dma("qg", f"wsl{pw}", lambda h, ob=ob: h.dma_start(
                    out=wsl[:, ob % 2], in_=d_wt[ob + 2]),
                    deps=TOK(sl_wsl[pw].wdeps()))
                sl_wsl[pw].write(t)
                t_wsl_tok[pw] = t


    emitter = S.finalize()
    with contextlib.ExitStack() as es2:
        for nm in list(ENGS) + sorted(S.qcnt):
            S.sems[nm] = es2.enter_context(nc.semaphore("sem_" + nm))
        with nc.Block() as block:
            block.sync(emitter("sync"))
            block.tensor(emitter("tensor"))
            block.vector(emitter("vector"))
            block.scalar(emitter("scalar"))
            block.gpsimd(emitter("gpsimd"))
    es.close()
    return nc


def _host_layout(inputs):
    import ml_dtypes
    x = np.asarray(inputs["L0_attn_x"], np.float32)
    xw = np.asarray(inputs["L0_attn_x_proj_w"], np.float32)
    dtw = np.asarray(inputs["L0_attn_dt_proj_w"], np.float32)
    alog = np.asarray(inputs["L0_attn_A_log"], np.float32)
    dsk = np.asarray(inputs["L0_attn_D"], np.float32)
    dtbv = np.asarray(inputs["L0_attn_dt_proj_b"], np.float32)
    W = np.asarray(inputs["L0_ffn_W"], np.float32)
    bb = np.asarray(inputs["L0_ffn_b"], np.float32)
    Wq = np.asarray(inputs["L0_ffn_Wq"], np.float32)
    Wk = np.asarray(inputs["L0_ffn_Wk"], np.float32)
    Aff = np.asarray(inputs["L0_ffn_A"], np.float32)
    Bm = np.asarray(inputs["L0_ffn_Bm"], np.float32)

    common = {}
    xw1 = xw[:128].T.reshape(NB, 128, 128).transpose(1, 0, 2)
    common["xw1"] = np.ascontiguousarray(
        xw1.reshape(128, 8, 4, 128).transpose(1, 0, 2, 3))
    common["xw2"] = np.ascontiguousarray(
        xw[128:160].T.reshape(NB, 128, 32).transpose(1, 0, 2).reshape(128, NB * 32))
    dtw1 = dtw.reshape(NB, 128, 128).transpose(2, 0, 1)
    common["dtw"] = np.ascontiguousarray(
        dtw1.reshape(128, 8, 4, 128).transpose(1, 0, 2, 3))
    common["alog"] = np.ascontiguousarray(
        alog.reshape(NB, 128, 16).transpose(1, 0, 2).reshape(128, NB * 16))
    common["dsb"] = np.ascontiguousarray(dsk.reshape(NB, 128).T)
    common["dtb"] = np.ascontiguousarray(dtbv.reshape(NB, 128).T)
    common["bcol"] = np.ascontiguousarray(bb.reshape(NB, 128).T)
    cst = np.zeros((128, 3), np.float32)
    cst[:, 0] = 1e-5
    cst[:, 1] = -0.5
    cst[:, 2] = 1.0 / np.sqrt(DK)
    common["cst"] = cst
    common["onesr"] = np.ones((65, 128), np.float32)
    common["onesb"] = np.ones((33, 128), ml_dtypes.bfloat16)
    common["onesc"] = np.ones((128, 1), np.float32)
    common["onesbc"] = np.ones((128, 1), ml_dtypes.bfloat16)
    common["ident"] = np.eye(128, dtype=np.float32)
    common["identb"] = np.eye(128, dtype=np.float32).astype(ml_dtypes.bfloat16)
    seg = np.zeros((128, 8), np.float32)
    for p in range(128):
        seg[p, p // 32] = 1.0
        seg[p, 4 + p // 32] = 1.0
    common["seg"] = seg.astype(ml_dtypes.bfloat16)
    rep = np.zeros((8, 128), np.float32)
    for c2 in range(128):
        rep[c2 // 16, c2] = 1.0
    common["rep"] = rep
    common["wq"] = np.ascontiguousarray(
        Wq.T.reshape(NB, 128, 256).transpose(1, 0, 2).reshape(128, NB * 256))
    common["wk"] = np.ascontiguousarray(
        Wk.T.reshape(NB, 128, 256).transpose(1, 0, 2).reshape(128, NB * 256))
    common["aff"] = np.ascontiguousarray(
        Aff.transpose(2, 0, 1).reshape(DM, 128).reshape(NB, 128, 128)
        .transpose(1, 0, 2).reshape(128, NB * 128))
    common["bm"] = np.ascontiguousarray(Bm.transpose(0, 2, 1).reshape(128, DM))
    common["wt"] = np.ascontiguousarray(
        W.T.reshape(NB, 128, NB, 128).transpose(2, 1, 0, 3))
    in_maps = []
    for b in range(8):
        m = dict(common)
        m["x"] = np.ascontiguousarray(
            x[b].T.reshape(NB, 128, NCH, TC).transpose(2, 1, 0, 3))
        in_maps.append(m)
    return in_maps


def kernel(**inputs):
    in_maps = _host_layout(inputs)
    nc = build_nc()
    res = run_bass_kernel_spmd(nc, in_maps, core_ids=list(range(8)))
    outs = []
    for b in range(8):
        o = np.asarray(res.results[b]["out"], np.float32)
        o = o.transpose(1, 2, 4, 0, 3).reshape(L, DM)
        outs.append(o)
    return np.stack(outs).astype(np.float32)


# version 12
# speedup vs baseline: 1.2195x; 1.2195x over previous
import contextlib
import numpy as np
import concourse.bass as bass
import concourse.mybir as mybir
from concourse.bass_utils import run_bass_kernel_spmd

F32 = mybir.dt.float32
F32R = mybir.dt.float32r
BF16 = mybir.dt.bfloat16
AF = mybir.ActivationFunctionType
OP = mybir.AluOpType
AX = mybir.AxisListType

DM = 4096
L = 2048
NB = 32
NST = 16
TC = 256
NCH = L // TC
DK = 32

ENGS = ("sync", "tensor", "vector", "scalar", "gpsimd")
SAFE_DIST = 2


class Sched:

    def __init__(self):
        self.plan = {e: [] for e in ENGS}
        self.cnt = {e: 0 for e in ENGS}
        self.qcnt = {}
        self.last_wait = {e: {} for e in ENGS}
        self.sems = {}
        self.used = {k: set() for k in ENGS}

    def _wait(self, eng, sem, val):
        if val <= 0:
            return
        if sem == eng and (self.cnt[eng] - val) > SAFE_DIST:
            return
        lw = self.last_wait[eng].get(sem, 0)
        if val <= lw:
            return
        self.last_wait[eng][sem] = val
        if sem in self.used:
            self.used[sem].add(val)
        self.plan[eng].append(("wait", sem, val))

    def op(self, eng, fn, deps=()):
        for (s, v) in deps:
            self._wait(eng, s, v)
        self.cnt[eng] += 1
        self.plan[eng].append(("op", fn, eng, self.cnt[eng]))
        return (eng, self.cnt[eng])

    def dma(self, q, sem, fn, deps=()):
        eng = "sync" if q == "qs" else "gpsimd"
        for (s, v) in deps:
            self._wait(eng, s, v)
        self.qcnt[sem] = self.qcnt.get(sem, 0) + 1
        self.plan[eng].append(("dma", fn, sem))
        return (sem, self.qcnt[sem] * 16)

    def raw(self, eng, fn):
        self.plan[eng].append(("raw", fn))

    def finalize(self):
        self.rank = {}
        for k, s in self.used.items():
            self.rank[k] = {v: i + 1 for i, v in enumerate(sorted(s))}
        fin = [(sem, n * 16) for sem, n in self.qcnt.items()]

        def emitter(eng):
            def run(h):
                for rec in self.plan[eng]:
                    if rec[0] == "wait":
                        _, sem, v = rec
                        if sem in self.rank:
                            v = self.rank[sem][v]
                        h.wait_ge(self.sems[sem], v)
                    elif rec[0] == "dma":
                        _, fn, sem = rec
                        fn(h).then_inc(self.sems[sem], 16)
                    elif rec[0] == "op":
                        _, fn, sem, idx = rec
                        inst = fn(h)
                        if idx in self.used[sem]:
                            inst.then_inc(self.sems[sem], 1)
                if eng == "sync":
                    for (sem, v) in fin:
                        h.wait_ge(self.sems[sem], v)
            return run
        return emitter


class Slot:
    def __init__(self):
        self.wr = None
        self.rd = {}

    def write(self, tok):
        self.wr = tok
        self.rd = {}

    def read(self, tok):
        self.rd[tok[0]] = tok

    def rdeps(self):
        return [self.wr] if self.wr else []

    def wdeps(self):
        d = list(self.rd.values())
        if self.wr:
            d.append(self.wr)
        return d


def TOK(*deps):
    out = []
    for d in deps:
        if d is None:
            continue
        if isinstance(d, list):
            out.extend(x for x in d if x)
        elif d:
            out.append(d)
    return out


_DEBUG_XMID = False
_N_CHUNKS = NCH
_SKIP_B = False
_B_LIMIT = 4
_B0_LIMIT = 99
_M_LIMIT = 4


def build_nc():
    nc = bass.Bass("TRN2", target_bir_lowering=False)
    d_x = nc.dram_tensor("x", [NCH, 128, NB, TC], F32R, kind="ExternalInput").ap()
    d_xw1 = nc.dram_tensor("xw1", [8, 128, 4, 128], F32R, kind="ExternalInput").ap()
    d_xw2 = nc.dram_tensor("xw2", [128, NB * 32], F32R, kind="ExternalInput").ap()
    d_dtw = nc.dram_tensor("dtw", [8, 128, 4, 128], F32R, kind="ExternalInput").ap()
    d_alog = nc.dram_tensor("alog", [128, NB * 16], F32, kind="ExternalInput").ap()
    d_dsb = nc.dram_tensor("dsb", [128, NB], F32, kind="ExternalInput").ap()
    d_dtb = nc.dram_tensor("dtb", [128, NB], F32, kind="ExternalInput").ap()
    d_ones = nc.dram_tensor("onesr", [65, 128], F32R, kind="ExternalInput").ap()
    d_onesb = nc.dram_tensor("onesb", [33, 128], BF16, kind="ExternalInput").ap()
    d_onesc = nc.dram_tensor("onesc", [128, 1], F32R, kind="ExternalInput").ap()
    d_onesbc = nc.dram_tensor("onesbc", [128, 1], BF16, kind="ExternalInput").ap()
    d_ident = nc.dram_tensor("ident", [128, 128], F32R, kind="ExternalInput").ap()
    d_identb = nc.dram_tensor("identb", [128, 128], BF16, kind="ExternalInput").ap()
    d_seg = nc.dram_tensor("seg", [128, 8], BF16, kind="ExternalInput").ap()
    d_rep = nc.dram_tensor("rep", [8, 128], F32R, kind="ExternalInput").ap()
    d_bcol = nc.dram_tensor("bcol", [128, NB], F32, kind="ExternalInput").ap()
    d_cst = nc.dram_tensor("cst", [128, 3], F32, kind="ExternalInput").ap()
    d_wq = nc.dram_tensor("wq", [128, NB * 256], F32, kind="ExternalInput").ap()
    d_wk = nc.dram_tensor("wk", [128, NB * 256], F32, kind="ExternalInput").ap()
    d_aff = nc.dram_tensor("aff", [128, NB * 128], F32, kind="ExternalInput").ap()
    d_bm = nc.dram_tensor("bm", [128, NB * 128], F32, kind="ExternalInput").ap()
    d_wt = nc.dram_tensor("wt", [NB, 128, NB, 128], F32, kind="ExternalInput").ap()
    d_xmid = nc.dram_tensor("xmid", [NCH, 128, NB, TC], F32R,
                            kind="ExternalOutput" if _DEBUG_XMID else "Internal").ap()
    d_out = nc.dram_tensor("out", [NB, 2, 4, 128, TC], F32, kind="ExternalOutput").ap()

    es = contextlib.ExitStack()
    sb = lambda name, shape, dt: es.enter_context(nc.sbuf_tensor(name, shape, dt))

    xbuf = sb("xbuf", [128, 2 * NB * TC], F32R)
    xln = sb("xln", [128, NB * TC], F32R)
    dtb16 = sb("dtb16", [128, NB * TC], BF16)
    hblk = sb("hblk", [128, NST * TC], BF16)
    s_xw1 = sb("s_xw1", [128, 1024], F32R)
    s_dtw = sb("s_dtw", [128, 1024], F32R)
    wk2 = sb("wk2", [128, 2048], F32R)
    wk3 = sb("wk3", [128, 2048], F32R)
    pool = sb("pool", [128, 2560], F32)
    prods = sb("prods", [128, 512], F32R)
    wqx = sb("wqx", [128, 4096], BF16)
    bbc = sb("bbc", [128, NST * TC], BF16)
    cbc = sb("cbc", [128, NST * TC], BF16)
    xp2f = sb("xp2f", [33, 16 * TC], BF16)
    s_xw2 = sb("s_xw2", [128, NB * 32], F32R)
    s_alog = sb("s_alog", [128, NB * 16], F32)
    hstate = sb("hstate", [128, NB * NST], F32)
    xp1s = sb("xp1s", [128, TC], F32R)
    xp2s = sb("xp2s", [32, TC], BF16)
    sqst = sb("sqst", [128, 2 * TC], BF16)
    s_D = sb("s_D", [128, NB], F32)
    s_dtb = sb("s_dtb", [128, NB], F32)
    s_bcol = sb("s_bcol", [128, NB], F32)
    s_cst = sb("s_cst", [128, 3], F32)
    s_ones = sb("s_ones", [65, 128], F32R)
    s_onesb = sb("s_onesb", [33, 128], BF16)
    s_onesc = sb("s_onesc", [128, 1], F32R)
    s_onesbc = sb("s_onesbc", [128, 1], BF16)
    s_ident = sb("s_ident", [128, 128], F32R)
    s_identb = sb("s_identb", [128, 128], BF16)
    s_seg = sb("s_seg", [128, 8], BF16)
    s_rep = sb("s_rep", [8, 128], F32R)
    rows_t = sb("rows_t", [1, 2 * TC], F32)
    rows_r = sb("rows_r", [1, 2 * TC], F32)
    scw = sb("scw", [68, TC], F32)
    wtst = sb("wtst", [8, TC], F32R)
    r_mx = sb("r_mx", [128, 1], F32)
    r_bias = sb("r_bias", [128, 1], F32)
    r_ex = sb("r_ex", [128, 8], F32)
    r_sum = sb("r_sum", [128, 1], F32)
    r_rec = sb("r_rec", [128, 1], F32)
    r_wsm = sb("r_wsm", [128, 8], F32)
    r_mean = rows_t[0:1, 0:TC]
    r_var = rows_t[0:1, TC:2 * TC]
    r_rstd = rows_r[0:1, 0:TC].bitcast(F32R)
    r_lnv = rows_r[0:1, TC:2 * TC].bitcast(F32R)
    r_mr = rows_r[0:1, TC:2 * TC].bitcast(F32R)
    sc1s = scw[0:4, :]
    sc2s = scw[64:68, :]
    wts = wtst[:]

    def xv_of(par):
        return xbuf[:, par * NB * TC:(par + 1) * NB * TC].rearrange(
            "p (b t) -> p b t", b=NB)
    xlnv = xln[:].rearrange("p (b t) -> p b t", b=NB)
    xw1v = s_xw1[:].rearrange("p (x b t) -> p x b t", x=2, b=4)
    dtwv = s_dtw[:].rearrange("p (x b t) -> p x b t", x=2, b=4)
    hblk_last = hblk[:].rearrange("p (n t) -> p n t", n=NST)[:, :, TC - 1]
    NPAR = 3
    a_scr = [pool[:, i * 256:(i + 1) * 256] for i in range(NPAR)]
    b_grp = [pool[:, 768:1792]] * 2
    prodb = prods[:].bitcast(BF16)
    e1_scr = [pool[:, 1792:2048]] * 2
    sqs = sqst[:]
    w_scr = [pool[:, 2048:2304]] * 2
    t1_scr = [pool[:, 2304:2560]] * 2
    res_v = [pool[:, i * 256:(i + 1) * 256] for i in range(2)]
    t1B = [pool[:, i * 256:(i + 1) * 256] for i in range(2)]
    outs_v = [pool[:, 512 + i * 256:512 + (i + 1) * 256] for i in range(2)]
    q_v = [pool[:, 1024 + i * 256:1024 + (i + 1) * 256] for i in range(2)]
    qk_v = [pool[:, 1536 + i * 128:1536 + (i + 1) * 128].bitcast(BF16)
            for i in range(2)]
    h_v = pool[:, 1792:2048]
    wh_v = pool[:, 2048:2560].bitcast(BF16)
    xln2h = xbuf[:].bitcast(BF16).rearrange("p (b t) -> p b t", b=NB)
    xmid_ch = xlnv
    wsl = dtb16[:].rearrange("p (w b o) -> p w b o", w=2, b=NB)
    wq_lo = hblk[:].rearrange("p (b t) -> p b t", b=16)
    wq_hi = wqx[:].rearrange("p (b t) -> p b t", b=16)
    wkA = wk2[:].bitcast(BF16).rearrange("p (b t) -> p b t", b=16)
    wkB = wk3[:].bitcast(BF16).rearrange("p (b t) -> p b t", b=16)
    affv = bbc[:].rearrange("p (b t) -> p b t", b=NB)
    bmv = cbc[:].rearrange("p (b t) -> p b t", b=NB)

    pb = [es.enter_context(nc.psum_tensor(f"pb{i}", [128, 512], F32)) for i in range(8)]

    S = Sched()
    sl_x = [Slot(), Slot()]
    sl_xln = Slot()
    sl_dt = Slot()
    sl_bc = Slot()
    sl_sq = [Slot(), Slot()]
    sl_a = [Slot() for _ in range(4)]
    sl_b = [Slot() for _ in range(4)]
    sl_prod = [Slot() for _ in range(4)]
    sl_pb = [Slot() for _ in range(8)]
    sl_xp1 = Slot()
    sl_xp2f = Slot()
    sl_rows = Slot()
    sl_wscr = [Slot(), Slot()]
    sl_e1 = [Slot(), Slot()]
    sl_t1 = [Slot(), Slot()]
    sl_xw1 = [Slot(), Slot()]
    sl_dtw = [Slot(), Slot()]

    t_w = {}
    for nm, dst, src in [
        ("xw2", s_xw2, d_xw2), ("alog", s_alog, d_alog), ("dsb", s_D, d_dsb),
        ("dtb", s_dtb, d_dtb), ("bcol", s_bcol, d_bcol),
        ("ones", s_ones, d_ones), ("onesb", s_onesb, d_onesb),
        ("ident", s_ident, d_ident), ("seg", s_seg, d_seg), ("rep", s_rep, d_rep),
        ("onesc", s_onesc, d_onesc), ("onesbc", s_onesbc, d_onesbc),
        ("cst", s_cst, d_cst), ("identb", s_identb, d_identb),
    ]:
        t_w[nm] = S.dma("qs", "w0", lambda h, dst=dst, src=src: h.dma_start(out=dst[:], in_=src))
    t_w = {k: ("w0", S.qcnt["w0"] * 16) for k in t_w}
    for c in range(2):
        tok = S.dma("qs", f"xb{c % 2}", lambda h, c=c: h.dma_start(
            out=xv_of(c % 2), in_=d_x[c]))
        sl_x[c % 2].write(tok)
    for qq in range(2):
        tok = S.dma("qs", f"xw1{qq}", lambda h, qq=qq: h.dma_start(out=xw1v[:, qq], in_=d_xw1[qq]))
        sl_xw1[qq].write(tok)
        tok = S.dma("qs", f"dtw{qq}", lambda h, qq=qq: h.dma_start(out=dtwv[:, qq], in_=d_dtw[qq]))
        sl_dtw[qq].write(tok)
    ta = S.op("scalar", lambda h: h.activation(pool[:, 0:512], s_alog[:], AF.Exp,
                                               scale=1.0, bias=0.0),
              deps=[t_w["alog"]])
    t_A = S.op("vector", lambda h: h.tensor_scalar_mul(s_alog[:], pool[:, 0:512], -1.0),
               deps=[ta])
    sl_a[0].write(t_A)
    sl_a[1].write(t_A)
    t_hs0 = S.op("vector", lambda h: h.memset(hstate[:], 0.0))

    stats_tok = {}
    stats_mm_tok = {}

    def emit_stats(c):
        par = c % 2
        xv = xv_of(par)
        for k in range(NB):
            tsq = S.op("scalar", lambda h, k=k, xv=xv: h.activation(
                sqs[:, (k % 2) * TC:(k % 2 + 1) * TC], xv[:, k].bitcast(F32),
                AF.Square, scale=1.0, bias=0.0),
                deps=TOK(sl_x[par].rdeps(), sl_sq[k % 2].wdeps()))
            sl_sq[k % 2].write(tsq)
            sl_x[par].read(tsq)
            tmm = S.op("tensor", lambda h, k=k, xv=xv: h.matmul(
                pb[0][0:1, 0:TC], s_onesc[:].bitcast(F32), xv[:, k].bitcast(F32),
                start=(k == 0), stop=(k == NB - 1)),
                deps=TOK([t_w["onesc"]], sl_x[par].rdeps(),
                         sl_pb[0].wdeps() if k == 0 else []))
            sl_x[par].read(tmm)
            if k == 0:
                sl_pb[0].write(tmm)
            tm2 = S.op("tensor", lambda h, k=k: h.matmul(
                pb[1][0:1, 0:TC], s_onesbc[:], sqs[:, (k % 2) * TC:(k % 2 + 1) * TC],
                start=(k == 0), stop=(k == NB - 1)),
                deps=TOK([t_w["onesbc"], tsq], sl_pb[1].wdeps() if k == 0 else []))
            sl_sq[k % 2].read(tm2)
            if k == 0:
                sl_pb[1].write(tm2)
            t_sx_done, t_sxx_done = tmm, tm2
        stats_mm_tok[c] = (t_sx_done, t_sxx_done)

    def emit_stats_fin(c):
        t_sx_done, t_sxx_done = stats_mm_tok[c]
        td = S.op("vector", lambda h: h.tensor_scalar_mul(
            r_mean, pb[0][0:1, 0:TC], 1.0 / DM),
            deps=TOK([t_sx_done], sl_rows.wdeps()))
        sl_rows.write(td)
        sl_pb[0].read(td)
        td2 = S.op("vector", lambda h: h.tensor_mul(r_var, r_mean, r_mean),
                   deps=[td])
        td3 = S.op("vector", lambda h: h.scalar_tensor_tensor(
            r_var, pb[1][0:1, 0:TC], 1.0 / DM, r_var, OP.mult, OP.subtract),
            deps=TOK([t_sxx_done, td2]))
        sl_pb[1].read(td3)
        td4 = S.op("scalar", lambda h: h.activation(r_lnv, r_var, AF.Ln,
                                                    scale=1.0, bias=s_cst[0:1, 0:1]), deps=TOK([td3, t_w['cst']]))
        td5 = S.op("scalar", lambda h: h.activation(r_rstd, r_lnv.bitcast(F32), AF.Exp,
                                                    scale=s_cst[0:1, 1:2], bias=0.0), deps=[td4])
        td6 = S.op("vector", lambda h: h.tensor_mul(r_mr, r_mean,
                                                    r_rstd.bitcast(F32)), deps=[td5])
        for _t in (td4, td5, td6):
            sl_rows.read(_t)
        tb1 = S.op("tensor", lambda h: h.matmul(pb[0][:, 0:TC], s_ones[0:1, :], r_rstd,
                                                start=True, stop=True),
                   deps=TOK([td5], sl_pb[0].wdeps()))
        sl_pb[0].write(tb1)
        sl_rows.read(tb1)
        tb2 = S.op("tensor", lambda h: h.matmul(pb[1][:, 0:TC], s_ones[0:1, :], r_mr,
                                                start=True, stop=True),
                   deps=TOK([td6], sl_pb[1].wdeps()))
        sl_pb[1].write(tb2)
        sl_rows.read(tb2)
        stats_tok[c] = (tb1, tb2)

    for c in range(_N_CHUNKS):
        par = c % 2
        xv = xv_of(par)
        if c == 0:
            emit_stats(0)
            emit_stats_fin(0)
        tb1, tb2 = stats_tok[c]
        t_xln = []
        for k in range(NB):
            tt1 = S.op("vector", lambda h, k=k, xv=xv: h.tensor_mul(
                t1_scr[k % 2], xv[:, k].bitcast(F32), pb[0][:, 0:TC]),
                deps=TOK([tb1], sl_t1[k % 2].wdeps(), sl_x[par].rdeps()))
            sl_x[par].read(tt1)
            sl_pb[0].read(tt1)
            tt2 = S.op("vector", lambda h, k=k: h.tensor_tensor(
                xlnv[:, k], t1_scr[k % 2], pb[1][:, 0:TC], OP.subtract),
                deps=TOK([tb2], sl_xln.wdeps() if k == 0 else []))
            sl_t1[k % 2].write(tt2)
            sl_pb[1].read(tt2)
            if k == 0:
                sl_xln.write(tt2)
            t_xln.append(tt2)
            tx2 = S.op("vector", lambda h, k=k, xv=xv: h.scalar_tensor_tensor(
                xv[:, k], xlnv[:, k].bitcast(F32), s_D[:, k:k + 1],
                xv[:, k].bitcast(F32), OP.mult, OP.add),
                deps=TOK([t_w["dsb"], tt2]))
            sl_x[par].read(tx2)
            sl_xln.read(tx2)
        for k in range(NB):
            qq = k // 4
            t = S.op("tensor", lambda h, k=k, qq=qq: h.matmul(
                pb[2][:, 0:TC], xw1v[:, qq % 2, k % 4], xlnv[:, k],
                start=(k == 0), stop=(k == NB - 1)),
                deps=TOK(sl_xw1[qq % 2].rdeps(), [t_xln[k]],
                         sl_pb[2].wdeps() if k == 0 else []))
            sl_xln.read(t)
            sl_xw1[qq % 2].read(t)
            if k == 0:
                sl_pb[2].write(t)
            t_xp1 = t
            if k % 4 == 3 and (c < NCH - 1 or qq < 7):
                nq = (qq + 1) % 8
                tok = S.dma("qs", f"xw1{nq % 2}", lambda h, nq=nq: h.dma_start(
                    out=xw1v[:, nq % 2], in_=d_xw1[nq]),
                    deps=TOK(sl_xw1[nq % 2].wdeps()))
                sl_xw1[nq % 2].write(tok)
        for k in range(NB):
            t = S.op("tensor", lambda h, k=k: h.matmul(
                pb[3][0:32, 0:TC], s_xw2[:, k * 32:(k + 1) * 32], xlnv[:, k],
                start=(k == 0), stop=(k == NB - 1)),
                deps=TOK([t_w["xw2"], t_xln[k]], sl_pb[3].wdeps() if k == 0 else []))
            sl_xln.read(t)
            if k == 0:
                sl_pb[3].write(t)
            t_xp2 = t
        tev1 = S.op("scalar", lambda h: h.activation(xp1s[:], pb[2][:, 0:TC], AF.Copy,
                                                     scale=1.0, bias=0.0),
                    deps=TOK([t_xp1], sl_xp1.wdeps()))
        sl_pb[2].read(tev1)
        sl_xp1.write(tev1)
        tev2 = S.op("scalar", lambda h: h.activation(xp2s[:], pb[3][0:32, 0:TC], AF.Copy,
                                                     scale=1.0, bias=0.0),
                    deps=TOK([t_xp2], sl_xp2f.wdeps()))
        sl_pb[3].read(tev2)
        tfl = S.dma("qg", "xpf", lambda h: h.dma_start(out=xp2f[0:1, :], in_=xp2s[0:16, :]),
                    deps=TOK([tev2]))
        tfl = S.dma("qg", "xpf", lambda h: h.dma_start(out=xp2f[32:33, :], in_=xp2s[16:32, :]),
                    deps=TOK([tev2]))
        sl_xp2f.write(tfl)
        t_dt = []
        for k in range(NB):
            qq = k // 4
            tz = S.op("tensor", lambda h, k=k, qq=qq: h.matmul(
                pb[4 + k % 2][:, 0:TC], dtwv[:, qq % 2, k % 4], xp1s[:],
                start=True, stop=True),
                deps=TOK(sl_dtw[qq % 2].rdeps(), [tev1], sl_pb[4 + k % 2].wdeps()))
            sl_pb[4 + k % 2].write(tz)
            sl_xp1.read(tz)
            sl_dtw[qq % 2].read(tz)
            te = S.op("scalar", lambda h, k=k: h.activation(
                e1_scr[k % 2], pb[4 + k % 2][:, 0:TC], AF.Exp,
                scale=1.0, bias=s_dtb[:, k:k + 1]),
                deps=TOK([t_w["dtb"], tz], sl_e1[k % 2].wdeps()))
            sl_pb[4 + k % 2].read(te)
            sl_e1[k % 2].write(te)
            tl = S.op("scalar", lambda h, k=k: h.activation(
                dtb16[:, k * TC:(k + 1) * TC], e1_scr[k % 2], AF.Ln,
                scale=1.0, bias=1.0),
                deps=TOK([te], sl_dt.wdeps() if k == 0 else []))
            sl_e1[k % 2].read(tl)
            if k == 0:
                sl_dt.write(tl)
            t_dt.append(tl)
            if k % 4 == 3 and (c < NCH - 1 or qq < 7):
                nq = (qq + 1) % 8
                tok = S.dma("qs", f"dtw{nq % 2}", lambda h, nq=nq: h.dma_start(
                    out=dtwv[:, nq % 2], in_=d_dtw[nq]),
                    deps=TOK(sl_dtw[nq % 2].wdeps()))
                sl_dtw[nq % 2].write(tok)
        t_bcB = [None] * 8
        t_bcC = [None] * 8
        for j in range(16):
            isB = j < 8
            tgt = bbc if isB else cbc
            jn = j % 8
            row = xp2f[0:1, :] if isB else xp2f[32:33, :]
            lhs1 = s_onesb[0:1, :] if isB else s_onesb[32:33, :]
            tb = S.op("tensor", lambda h, jn=jn, row=row, lhs1=lhs1, j=j: h.matmul(
                pb[2 + j % 2][:, 0:2 * TC], lhs1, row[:, jn * 2 * TC:(jn + 1) * 2 * TC],
                start=True, stop=True),
                deps=TOK([tfl, t_w["onesb"]], sl_pb[2 + j % 2].wdeps()))
            sl_pb[2 + j % 2].write(tb)
            sl_xp2f.read(tb)
            tv = S.op("scalar", lambda h, tgt=tgt, jn=jn, j=j: h.activation(
                tgt[:, jn * 2 * TC:(jn + 1) * 2 * TC], pb[2 + j % 2][:, 0:2 * TC],
                AF.Copy, scale=1.0, bias=0.0),
                deps=TOK([tb], sl_bc.wdeps() if j == 0 else []))
            sl_pb[2 + j % 2].read(tv)
            if j == 0:
                sl_bc.write(tv)
            if isB:
                t_bcB[jn] = tv
            else:
                t_bcC[jn] = tv
        t_bcast = t_bcC[7]
        pend_ty = []
        for k in range(NB):
            kp = k % 2
            tw = S.op("vector", lambda h, k=k, kp=kp: h.tensor_tensor(
                w_scr[kp], dtb16[:, k * TC:(k + 1) * TC], xlnv[:, k].bitcast(F32),
                OP.mult),
                deps=TOK([t_dt[k]], sl_wscr[kp].wdeps()))
            sl_dt.read(tw)
            sl_xln.read(tw)
            sl_wscr[kp].write(tw)
            t_aexp = [None] * NST
            t_bb = [None] * 4
            t_scan = [None] * NST
            t_cm = [None] * 4
            t_id = [None] * NST

            def emit_a(n, k=k, t_aexp=t_aexp):
                p4 = n % NPAR
                t = S.op("scalar", lambda h, k=k, n=n, p4=p4: h.activation(
                    a_scr[p4], dtb16[:, k * TC:(k + 1) * TC], AF.Exp,
                    scale=s_alog[:, k * NST + n:k * NST + n + 1], bias=0.0),
                    deps=TOK([t_A, t_dt[k]], sl_a[p4].wdeps()))
                sl_a[p4].write(t)
                sl_dt.read(t)
                t_aexp[n] = t

            def emit_bgrp(g, k=k, kp=kp, t_bb=t_bb, tw=tw):
                gp = g % 2
                t = S.op("vector", lambda h, g=g, gp=gp, kp=kp: h.tensor_tensor(
                    b_grp[gp].rearrange("p (x t) -> p x t", x=4),
                    w_scr[kp].unsqueeze(1).broadcast_to((128, 4, TC)),
                    bbc[:, g * 4 * TC:(g + 1) * 4 * TC].rearrange(
                        "p (x t) -> p x t", x=4), OP.mult),
                    deps=TOK([tw, t_bcB[g * 2 + 1]], sl_b[gp].wdeps()))
                sl_wscr[kp].read(t)
                sl_bc.read(t)
                sl_b[gp].write(t)
                t_bb[g] = t

            def emit_scan(n, k=k, t_scan=t_scan, t_aexp=t_aexp, t_bb=t_bb):
                p4 = n % NPAR
                g, gp = n // 4, (n // 4) % 2
                t = S.op("vector", lambda h, k=k, n=n, p4=p4, gp=gp: h.tensor_tensor_scan(
                    hblk[:, n * TC:(n + 1) * TC], a_scr[p4],
                    b_grp[gp][:, (n % 4) * TC:(n % 4 + 1) * TC],
                    hstate[:, k * NST + n:k * NST + n + 1], OP.mult, OP.add),
                    deps=TOK([t_aexp[n], t_bb[g], t_hs0]))
                sl_a[p4].read(t)
                sl_b[gp].read(t)
                t_scan[n] = t

            def emit_cmgrp(g, t_cm=t_cm, t_scan=t_scan):
                t = S.op("vector", lambda h, g=g: h.tensor_tensor(
                    prodb, hblk[:, g * 4 * TC:(g + 1) * 4 * TC],
                    cbc[:, g * 4 * TC:(g + 1) * 4 * TC], OP.mult),
                    deps=TOK([t_scan[g * 4 + 3], t_bcC[g * 2 + 1]], sl_prod[0].wdeps()))
                sl_prod[0].write(t)
                sl_bc.read(t)
                t_cm[g] = t

            def emit_id(n, k=k, t_id=t_id, t_cm=t_cm):
                t = S.op("tensor", lambda h, k=k, n=n: h.matmul(
                    pb[6 + k % 2][:, 0:TC], s_identb[:],
                    prodb[:, (n % 4) * TC:(n % 4 + 1) * TC],
                    start=(n == 0), stop=(n == NST - 1)),
                    deps=TOK([t_w["identb"], t_cm[n // 4]],
                             sl_pb[6 + k % 2].wdeps() if n == 0 else []))
                sl_prod[0].read(t)
                if n == 0:
                    sl_pb[6 + k % 2].write(t)
                t_id[n] = t

            emit_a(0)
            emit_bgrp(0)
            emit_a(1)
            for n in range(NST):
                emit_scan(n)
                if n + 2 < NST:
                    emit_a(n + 2)
                if n % 4 == 3 and n // 4 + 1 < 4:
                    emit_bgrp(n // 4 + 1)
                if n % 4 == 0 and n >= 4:
                    g = n // 4 - 1
                    emit_cmgrp(g)
                    for j in range(4):
                        emit_id(g * 4 + j)
            emit_cmgrp(3)
            for j in range(4):
                emit_id(12 + j)
            S.op("vector", lambda h, k=k: h.tensor_copy(
                hstate[:, k * NST:(k + 1) * NST], hblk_last),
                deps=TOK([t_scan[NST - 1]]))
            pend_ty.append((k, t_id[NST - 1], xv))
            if len(pend_ty) > 1 or k == NB - 1:
                for (k2, tok_id, xv2) in (pend_ty[:-1] if k < NB - 1 else pend_ty):
                    ty = S.op("vector", lambda h, k2=k2, xv2=xv2: h.tensor_tensor(
                        xv2[:, k2], xv2[:, k2].bitcast(F32),
                        pb[6 + k2 % 2][:, 0:TC], OP.add),
                        deps=TOK([tok_id]))
                    sl_pb[6 + k2 % 2].read(ty)
                    sl_x[par].read(ty)
                pend_ty[:] = [] if k == NB - 1 else pend_ty[-1:]
            if k == 12 and c + 1 < NCH:
                emit_stats(c + 1)
            if k == 26 and c + 1 < NCH:
                emit_stats_fin(c + 1)
        tst = S.dma("qg", f"xms{par}", lambda h, c=c, xv=xv: h.dma_start(out=d_xmid[c], in_=xv),
                    deps=TOK([sl_x[par].rd.get("vector")]))
        sl_x[par].read(tst)
        if c + 2 < NCH:
            tok = S.dma("qs", f"xb{par}", lambda h, c=c: h.dma_start(
                out=xv_of(c % 2), in_=d_x[c + 2]), deps=TOK(sl_x[par].wdeps()))
            sl_x[par].write(tok)

    allbar = [(e, S.cnt[e]) for e in ENGS if S.cnt[e] > 0]
    allbar += [(sem, n * 16) for sem, n in S.qcnt.items()]

    t_wq = S.dma("qg", "wB", lambda h: h.dma_start(out=hblk[:], in_=d_wq[:, 0:16 * 256]),
                 deps=allbar)
    t_wq2 = S.dma("qg", "wB", lambda h: h.dma_start(out=wqx[:], in_=d_wq[:, 16 * 256:NB * 256]),
                  deps=allbar)
    t_wkA = S.dma("qg", "wB", lambda h: h.dma_start(out=wk2[:].bitcast(BF16),
                                                    in_=d_wk[:, 0:16 * 256]), deps=allbar)
    t_wkB = S.dma("qg", "wB", lambda h: h.dma_start(out=wk3[:].bitcast(BF16),
                                                    in_=d_wk[:, 16 * 256:32 * 256]),
                  deps=allbar)
    t_aff = S.dma("qg", "wB", lambda h: h.dma_start(out=bbc[:], in_=d_aff),
                  deps=allbar)
    t_bm = S.dma("qg", "wB", lambda h: h.dma_start(out=cbc[:], in_=d_bm),
                 deps=allbar)
    t_wq = t_wkA = t_wkB = t_aff = t_bm = ("wB", S.qcnt["wB"] * 16)
    sl_xl2 = Slot()
    sl_wh = Slot()
    sl_wsl = [Slot(), Slot()]
    sl_res = [Slot(), Slot()]
    sl_outs = [Slot(), Slot()]
    sl_xmc = Slot()
    sl_qv = [Slot(), Slot()]
    sl_qkv = [Slot(), Slot()]
    sl_hv = Slot()
    sl_sc = Slot()
    sl_wts = Slot()

    prev_soft = {"tex": None, "twm": None, "ttb": None}
    for half in range(0 if _SKIP_B else (2 if _B_LIMIT >= 4 else 1)):
        for cc in range(4 if _B_LIMIT >= 2 else 0):
            g = half * 4 + cc
            tld = S.dma("qs", "xmc", lambda h, g=g: h.dma_start(out=xmid_ch, in_=d_xmid[g]),
                        deps=TOK(allbar if (half == 0 and cc == 0) else [],
                                 sl_xmc.wdeps()))
            sl_xmc.write(tld)
            for k in range(NB):
                tsq = S.op("scalar", lambda h, k=k: h.activation(
                    sqs[:, (k % 2) * TC:(k % 2 + 1) * TC], xmid_ch[:, k].bitcast(F32),
                    AF.Square, scale=1.0, bias=0.0),
                    deps=TOK([tld], sl_sq[k % 2].wdeps(),
                             allbar if (half == 0 and cc == 0 and k == 0) else []))
                sl_sq[k % 2].write(tsq)
                sl_xmc.read(tsq)
                tmm = S.op("tensor", lambda h, k=k: h.matmul(
                    pb[0][0:1, 0:TC], s_onesc[:], xmid_ch[:, k],
                    start=(k == 0), stop=(k == NB - 1)),
                    deps=TOK([tld], sl_pb[0].wdeps() if k == 0 else [],
                             allbar if (half == 0 and cc == 0 and k == 0) else []))
                sl_xmc.read(tmm)
                if k == 0:
                    sl_pb[0].write(tmm)
                tm2 = S.op("tensor", lambda h, k=k: h.matmul(
                    pb[1][0:1, 0:TC], s_onesbc[:], sqs[:, (k % 2) * TC:(k % 2 + 1) * TC],
                    start=(k == 0), stop=(k == NB - 1)),
                    deps=TOK([tsq], sl_pb[1].wdeps() if k == 0 else []))
                sl_sq[k % 2].read(tm2)
                if k == 0:
                    sl_pb[1].write(tm2)
                t_sx_done, t_sxx_done = tmm, tm2
            td = S.op("vector", lambda h: h.tensor_scalar_mul(
                r_mean, pb[0][0:1, 0:TC], 1.0 / DM),
                deps=TOK([t_sx_done], sl_rows.wdeps(),
                         allbar if (half == 0 and cc == 0) else []))
            sl_rows.write(td)
            sl_pb[0].read(td)
            td2 = S.op("vector", lambda h: h.tensor_mul(r_var, r_mean, r_mean),
                       deps=[td])
            td3 = S.op("vector", lambda h: h.scalar_tensor_tensor(
                r_var, pb[1][0:1, 0:TC], 1.0 / DM, r_var, OP.mult, OP.subtract),
                deps=TOK([t_sxx_done, td2]))
            sl_pb[1].read(td3)
            td4 = S.op("scalar", lambda h: h.activation(r_lnv, r_var, AF.Ln,
                                                        scale=1.0, bias=s_cst[0:1, 0:1]),
                       deps=TOK([td3, t_w['cst']]))
            td5 = S.op("scalar", lambda h: h.activation(r_rstd, r_lnv.bitcast(F32), AF.Exp,
                                                        scale=s_cst[0:1, 1:2], bias=0.0),
                       deps=[td4])
            td6 = S.op("vector", lambda h: h.tensor_mul(
                r_mr, r_mean, r_rstd.bitcast(F32)), deps=[td5])
            for _t in (td4, td5, td6):
                sl_rows.read(_t)
            tb1 = S.op("tensor", lambda h: h.matmul(pb[0][:, 0:TC], s_ones[0:1, :],
                                                    r_rstd, start=True, stop=True),
                       deps=TOK([td5], sl_pb[0].wdeps()))
            sl_pb[0].write(tb1)
            sl_rows.read(tb1)
            tb2 = S.op("tensor", lambda h: h.matmul(pb[1][:, 0:TC], s_ones[0:1, :], r_mr,
                                                    start=True, stop=True),
                       deps=TOK([td6], sl_pb[1].wdeps()))
            sl_pb[1].write(tb2)
            sl_rows.read(tb2)
            if _B0_LIMIT < 2:
                continue
            t_x2 = []
            for k in range(NB):
                tt1 = S.op("vector", lambda h, k=k: h.tensor_mul(
                    t1B[k % 2], xmid_ch[:, k].bitcast(F32), pb[0][:, 0:TC]),
                    deps=TOK([tb1], sl_t1[k % 2].wdeps(), sl_res[k % 2].wdeps()))
                sl_xmc.read(tt1)
                sl_pb[0].read(tt1)
                sl_res[k % 2].read(tt1)
                tt2 = S.op("vector", lambda h, k=k, cc=cc: h.tensor_tensor(
                    xln2h[:, k, cc * TC:(cc + 1) * TC], t1B[k % 2],
                    pb[1][:, 0:TC], OP.subtract),
                    deps=TOK([tb2], sl_xl2.wdeps() if (k == 0 and cc == 0) else []))
                sl_t1[k % 2].write(tt2)
                sl_res[k % 2].read(tt2)
                sl_pb[1].read(tt2)
                if k == 0 and cc == 0:
                    sl_xl2.write(tt2)
                t_x2.append(tt2)
            if _B0_LIMIT < 3:
                continue
            for k in range(NB):
                wq_src = wq_lo if k < 16 else wq_hi
                tq1 = S.op("tensor", lambda h, k=k, cc=cc, wq_src=wq_src: h.matmul(
                    pb[2][:, 0:TC], wq_src[:, k % 16, 0:128],
                    xln2h[:, k, cc * TC:(cc + 1) * TC],
                    start=(k == 0), stop=(k == NB - 1)),
                    deps=TOK([t_wq, t_x2[k]], sl_pb[2].wdeps() if k == 0 else []))
                sl_xl2.read(tq1)
                if k == 0:
                    sl_pb[2].write(tq1)
                tq2 = S.op("tensor", lambda h, k=k, cc=cc, wq_src=wq_src: h.matmul(
                    pb[3][:, 0:TC], wq_src[:, k % 16, 128:256],
                    xln2h[:, k, cc * TC:(cc + 1) * TC],
                    start=(k == 0), stop=(k == NB - 1)),
                    deps=TOK([t_wq, t_x2[k]], sl_pb[3].wdeps() if k == 0 else []))
                if k == 0:
                    sl_pb[3].write(tq2)
                wk_src = wkA if k < 16 else wkB
                tk1 = S.op("tensor", lambda h, k=k, cc=cc, wk_src=wk_src: h.matmul(
                    pb[4][:, 0:TC], wk_src[:, k % 16, 0:128],
                    xln2h[:, k, cc * TC:(cc + 1) * TC],
                    start=(k == 0), stop=(k == NB - 1)),
                    deps=TOK([t_wkA, t_wkB, t_x2[k]],
                             sl_pb[4].wdeps() if k == 0 else []))
                if k == 0:
                    sl_pb[4].write(tk1)
                tk2 = S.op("tensor", lambda h, k=k, cc=cc, wk_src=wk_src: h.matmul(
                    pb[5][:, 0:TC], wk_src[:, k % 16, 128:256],
                    xln2h[:, k, cc * TC:(cc + 1) * TC],
                    start=(k == 0), stop=(k == NB - 1)),
                    deps=TOK([t_wkA, t_wkB, t_x2[k]],
                             sl_pb[5].wdeps() if k == 0 else []))
                if k == 0:
                    sl_pb[5].write(tk2)
                th = S.op("tensor", lambda h, k=k, cc=cc: h.matmul(
                    pb[6][:, 0:TC], affv[:, k], xln2h[:, k, cc * TC:(cc + 1) * TC],
                    start=(k == 0), stop=(k == NB - 1)),
                    deps=TOK([t_aff, t_x2[k]], sl_pb[6].wdeps() if k == 0 else []))
                if k == 0:
                    sl_pb[6].write(th)
                t_qd, t_q2d, t_kd, t_k2d, t_hd = tq1, tq2, tk1, tk2, th
            if _B0_LIMIT < 4:
                continue
            t_sc_ev = []
            for g2 in range(2):
                tevq = S.op("scalar", lambda h, g2=g2: h.activation(
                    q_v[g2], pb[2 + g2][:, 0:TC], AF.Copy, scale=1.0, bias=0.0),
                    deps=TOK([t_qd if g2 == 0 else t_q2d], sl_qv[g2].wdeps()))
                sl_pb[2 + g2].read(tevq)
                sl_qv[g2].write(tevq)
                tqk = S.op("vector", lambda h, g2=g2: h.tensor_tensor(
                    qk_v[g2], pb[4 + g2][:, 0:TC], q_v[g2], OP.mult),
                    deps=TOK([tevq, t_kd if g2 == 0 else t_k2d], sl_qkv[g2].wdeps()))
                sl_pb[4 + g2].read(tqk)
                sl_qv[g2].read(tqk)
                sl_qkv[g2].write(tqk)
                bank = pb[7] if g2 == 0 else pb[0]
                base = 0 if g2 == 0 else 64
                bslot = sl_pb[7] if g2 == 0 else sl_pb[0]
                tsc_mm = S.op("tensor", lambda h, g2=g2, base=base, bank=bank: h.matmul(
                    bank[base:base + 4, 0:TC], s_seg[:, g2 * 4:(g2 + 1) * 4],
                    qk_v[g2], start=True, stop=True),
                    deps=TOK([t_w["seg"], tqk], bslot.wdeps()))
                sl_qkv[g2].read(tsc_mm)
                bslot.write(tsc_mm)
                tevs = S.op("scalar", lambda h, g2=g2, base=base, bank=bank: h.activation(
                    (sc1s if g2 == 0 else sc2s)[:], bank[base:base + 4, 0:TC],
                    AF.Copy, scale=1.0, bias=0.0),
                    deps=TOK([tsc_mm], sl_sc.wdeps() if g2 == 0 else []))
                bslot.read(tevs)
                if g2 == 0:
                    sl_sc.write(tevs)
                t_sc_ev.append(tevs)
            if _B0_LIMIT < 5:
                continue
            for ts in range(2):
                ttr1 = S.op("tensor", lambda h, ts=ts: h.transpose(
                    pb[1][:, ts * 8:ts * 8 + 4], sc1s[:, ts * 128:(ts + 1) * 128],
                    s_ident[0:4, 0:4].bitcast(F32)),
                    deps=TOK(t_sc_ev, sl_pb[1].wdeps()))
                sl_pb[1].write(ttr1)
                sl_sc.read(ttr1)
                ttr2 = S.op("tensor", lambda h, ts=ts: h.transpose(
                    pb[1][:, ts * 8 + 4:ts * 8 + 8], sc2s[:, ts * 128:(ts + 1) * 128],
                    s_ident[64:68, 64:68].bitcast(F32)),
                    deps=TOK(t_sc_ev))
                sl_sc.read(ttr2)
                if _B0_LIMIT < 6:
                    continue
                tmx = S.op("vector", lambda h, ts=ts: h.reduce_max(
                    r_mx[:], pb[1][:, ts * 8:ts * 8 + 8], axis=AX.X),
                    deps=TOK([ttr2, prev_soft["tex"]]))
                sl_pb[1].read(tmx)
                tbs = S.op("vector", lambda h: h.tensor_scalar_mul(
                    r_bias[:], r_mx[:], -1.0 / float(np.sqrt(DK))),
                    deps=TOK([tmx, prev_soft["tex"]]))
                if _B0_LIMIT < 7:
                    continue
                tex = S.op("scalar", lambda h, ts=ts: h.activation(
                    r_ex[:], pb[1][:, ts * 8:ts * 8 + 8], AF.Exp,
                    scale=s_cst[:, 2:3], bias=r_bias[:]),
                    deps=TOK([tbs, prev_soft["twm"]]))
                sl_pb[1].read(tex)
                if _B0_LIMIT < 8:
                    continue
                tsm = S.op("vector", lambda h: h.reduce_sum(
                    r_sum[:], r_ex[:], axis=AX.X), deps=TOK([tex]))
                trc = S.op("vector", lambda h: h.reciprocal(r_rec[:], r_sum[:]),
                           deps=[tsm])
                twm = S.op("vector", lambda h: h.tensor_scalar(
                    r_wsm[:].bitcast(BF16)[:, 0:8], r_ex[:], r_rec[:], None, OP.mult),
                    deps=TOK([trc, tex, prev_soft["ttb"]]))
                if _B0_LIMIT < 9:
                    continue
                ttb = S.op("tensor", lambda h: h.transpose(
                    pb[0][:].bitcast(BF16)[0:8, 0:128], r_wsm[:].bitcast(BF16)[:, 0:8],
                    s_identb[:]),
                    deps=TOK([twm, t_w["identb"]], sl_pb[0].wdeps()))
                sl_pb[0].write(ttb)
                tevw = S.op("scalar", lambda h, ts=ts: h.activation(
                    wts[:, ts * 128:(ts + 1) * 128],
                    pb[0][:].bitcast(BF16)[0:8, 0:128], AF.Copy,
                    scale=1.0, bias=0.0),
                    deps=TOK([ttb], sl_wts.wdeps() if ts == 0 else []))
                sl_pb[0].read(tevw)
                if ts == 0:
                    sl_wts.write(tevw)
                prev_soft = {"tex": tex, "twm": twm, "ttb": ttb}
                t_wts = tevw
            if _B0_LIMIT < 10:
                continue
            twbc = S.op("tensor", lambda h: h.matmul(
                pb[0][:, 0:TC], s_rep[:], wts, start=True, stop=True),
                deps=TOK([t_w["rep"], t_wts], sl_pb[0].wdeps()))
            sl_pb[0].write(twbc)
            sl_wts.read(twbc)
            tevh = S.op("scalar", lambda h: h.activation(
                h_v, pb[6][:, 0:TC], AF.Copy, scale=1.0, bias=0.0),
                deps=TOK([t_hd], sl_hv.wdeps()))
            sl_pb[6].read(tevh)
            sl_hv.write(tevh)
            twh = S.op("vector", lambda h, cc=cc: h.tensor_tensor(
                wh_v[:, cc * TC:(cc + 1) * TC], h_v, pb[0][:, 0:TC], OP.mult),
                deps=TOK([tevh, twbc], sl_wh.wdeps() if cc == 0 else []))
            sl_pb[0].read(twh)
            sl_hv.read(twh)
            if cc == 0:
                sl_wh.write(twh)
            t_wh = twh
        t_wsl_tok = [None, None]
        for ob in range(2 if _B_LIMIT >= 3 else 0):
            t = S.dma("qg", f"wsl{ob % 2}", lambda h, ob=ob: h.dma_start(
                out=wsl[:, ob % 2], in_=d_wt[ob]),
                deps=TOK(sl_wsl[ob % 2].wdeps(),
                         allbar if (half == 0 and ob == 0) else []))
            sl_wsl[ob % 2].write(t)
            t_wsl_tok[ob % 2] = t
        for ob in range(NB if _B_LIMIT >= 3 else 0):
            pw = ob % 2
            t_res = [None] * 4
            for tcx in range(2):
                t_res[tcx] = S.dma("qs", f"res{tcx % 2}", lambda h, ob=ob, half=half, tcx=tcx: h.dma_start(
                    out=res_v[tcx % 2].bitcast(F32R),
                    in_=d_xmid[half * 4 + tcx, :, ob, :]),
                    deps=TOK(sl_res[tcx % 2].wdeps()))
                sl_res[tcx % 2].write(t_res[tcx])
            for tch in range(4 if _M_LIMIT >= 2 else 0):
                pk = tch % 2
                if tch >= 2:
                    t_res[tch] = S.dma("qs", f"res{tch % 2}", lambda h, ob=ob, half=half, tch=tch: h.dma_start(
                        out=res_v[tch % 2].bitcast(F32R),
                        in_=d_xmid[half * 4 + tch, :, ob, :]),
                        deps=TOK(sl_res[tch % 2].wdeps()))
                    sl_res[tch % 2].write(t_res[tch])
                t_last = None
                for k in range(NB):
                    t_last = S.op("tensor", lambda h, k=k, ob=ob, tch=tch, pk=pk: h.matmul(
                        pb[2 + pk][:, 0:TC], wsl[:, ob % 2, k],
                        xln2h[:, k, tch * TC:(tch + 1) * TC],
                        start=(k == 0), stop=False),
                        deps=TOK([t_wsl_tok[pw] if k == 0 else None,
                                  t_wh if k == 0 else None],
                                 sl_pb[2 + pk].wdeps() if k == 0 else []))
                    if k == 0:
                        sl_pb[2 + pk].write(t_last)
                    sl_wsl[ob % 2].read(t_last)
                    sl_xl2.read(t_last)
                if _M_LIMIT >= 3:
                    t_last = S.op("tensor", lambda h, ob=ob, tch=tch, pk=pk: h.matmul(
                        pb[2 + pk][:, 0:TC], bmv[:, ob],
                        wh_v[:, tch * TC:(tch + 1) * TC],
                        start=False, stop=True),
                        deps=TOK([t_bm, t_wh]))
                    sl_wh.read(t_last)
                else:
                    t_last = S.op("tensor", lambda h, ob=ob, tch=tch, pk=pk: h.matmul(
                        pb[2 + pk][:, 0:TC], s_ident[:],
                        prods[:, 0:TC], start=False, stop=True),
                        deps=TOK([t_w["ident"]]))
                if _M_LIMIT >= 4:
                    tev = S.op("vector", lambda h, ob=ob, tch=tch, pk=pk:
                               h.scalar_tensor_tensor(
                        outs_v[pk], pb[2 + pk][:, 0:TC], s_bcol[:, ob:ob + 1],
                        res_v[tch % 2].bitcast(F32R).bitcast(F32),
                        OP.add, OP.add),
                        deps=TOK([t_last, t_res[tch], t_w["bcol"]],
                                 sl_outs[pk].wdeps()))
                else:
                    tev = S.op("vector", lambda h, ob=ob, tch=tch, pk=pk:
                               h.tensor_copy(outs_v[pk], pb[2 + pk][:, 0:TC]),
                        deps=TOK([t_last, t_res[tch]], sl_outs[pk].wdeps()))
                sl_pb[2 + pk].read(tev)
                sl_res[tch % 2].read(tev)
                sl_outs[pk].write(tev)
                tso = S.dma("qs", f"st{pk}", lambda h, ob=ob, half=half, tch=tch, pk=pk:
                            h.dma_start(out=d_out[ob, half, tch], in_=outs_v[pk]),
                            deps=TOK([tev]))
                sl_outs[pk].read(tso)
            if ob + 2 < NB:
                t = S.dma("qg", f"wsl{pw}", lambda h, ob=ob: h.dma_start(
                    out=wsl[:, ob % 2], in_=d_wt[ob + 2]),
                    deps=TOK(sl_wsl[pw].wdeps()))
                sl_wsl[pw].write(t)
                t_wsl_tok[pw] = t


    emitter = S.finalize()
    with contextlib.ExitStack() as es2:
        for nm in list(ENGS) + sorted(S.qcnt):
            S.sems[nm] = es2.enter_context(nc.semaphore("sem_" + nm))
        with nc.Block() as block:
            block.sync(emitter("sync"))
            block.tensor(emitter("tensor"))
            block.vector(emitter("vector"))
            block.scalar(emitter("scalar"))
            block.gpsimd(emitter("gpsimd"))
    es.close()
    return nc


def _host_layout(inputs):
    import ml_dtypes
    x = np.asarray(inputs["L0_attn_x"], np.float32)
    xw = np.asarray(inputs["L0_attn_x_proj_w"], np.float32)
    dtw = np.asarray(inputs["L0_attn_dt_proj_w"], np.float32)
    alog = np.asarray(inputs["L0_attn_A_log"], np.float32)
    dsk = np.asarray(inputs["L0_attn_D"], np.float32)
    dtbv = np.asarray(inputs["L0_attn_dt_proj_b"], np.float32)
    W = np.asarray(inputs["L0_ffn_W"], np.float32)
    bb = np.asarray(inputs["L0_ffn_b"], np.float32)
    Wq = np.asarray(inputs["L0_ffn_Wq"], np.float32)
    Wk = np.asarray(inputs["L0_ffn_Wk"], np.float32)
    Aff = np.asarray(inputs["L0_ffn_A"], np.float32)
    Bm = np.asarray(inputs["L0_ffn_Bm"], np.float32)

    common = {}
    xw1 = xw[:128].T.reshape(NB, 128, 128).transpose(1, 0, 2)
    common["xw1"] = np.ascontiguousarray(
        xw1.reshape(128, 8, 4, 128).transpose(1, 0, 2, 3))
    common["xw2"] = np.ascontiguousarray(
        xw[128:160].T.reshape(NB, 128, 32).transpose(1, 0, 2).reshape(128, NB * 32))
    dtw1 = dtw.reshape(NB, 128, 128).transpose(2, 0, 1)
    common["dtw"] = np.ascontiguousarray(
        dtw1.reshape(128, 8, 4, 128).transpose(1, 0, 2, 3))
    common["alog"] = np.ascontiguousarray(
        alog.reshape(NB, 128, 16).transpose(1, 0, 2).reshape(128, NB * 16))
    common["dsb"] = np.ascontiguousarray(dsk.reshape(NB, 128).T)
    common["dtb"] = np.ascontiguousarray(dtbv.reshape(NB, 128).T)
    common["bcol"] = np.ascontiguousarray(bb.reshape(NB, 128).T)
    cst = np.zeros((128, 3), np.float32)
    cst[:, 0] = 1e-5
    cst[:, 1] = -0.5
    cst[:, 2] = 1.0 / np.sqrt(DK)
    common["cst"] = cst
    common["onesr"] = np.ones((65, 128), np.float32)
    common["onesb"] = np.ones((33, 128), ml_dtypes.bfloat16)
    common["onesc"] = np.ones((128, 1), np.float32)
    common["onesbc"] = np.ones((128, 1), ml_dtypes.bfloat16)
    common["ident"] = np.eye(128, dtype=np.float32)
    common["identb"] = np.eye(128, dtype=np.float32).astype(ml_dtypes.bfloat16)
    seg = np.zeros((128, 8), np.float32)
    for p in range(128):
        seg[p, p // 32] = 1.0
        seg[p, 4 + p // 32] = 1.0
    common["seg"] = seg.astype(ml_dtypes.bfloat16)
    rep = np.zeros((8, 128), np.float32)
    for c2 in range(128):
        rep[c2 // 16, c2] = 1.0
    common["rep"] = rep
    common["wq"] = np.ascontiguousarray(
        Wq.T.reshape(NB, 128, 256).transpose(1, 0, 2).reshape(128, NB * 256))
    common["wk"] = np.ascontiguousarray(
        Wk.T.reshape(NB, 128, 256).transpose(1, 0, 2).reshape(128, NB * 256))
    common["aff"] = np.ascontiguousarray(
        Aff.transpose(2, 0, 1).reshape(DM, 128).reshape(NB, 128, 128)
        .transpose(1, 0, 2).reshape(128, NB * 128))
    common["bm"] = np.ascontiguousarray(Bm.transpose(0, 2, 1).reshape(128, DM))
    common["wt"] = np.ascontiguousarray(
        W.T.reshape(NB, 128, NB, 128).transpose(2, 1, 0, 3))
    in_maps = []
    for b in range(8):
        m = dict(common)
        m["x"] = np.ascontiguousarray(
            x[b].T.reshape(NB, 128, NCH, TC).transpose(2, 1, 0, 3))
        in_maps.append(m)
    return in_maps


def kernel(**inputs):
    in_maps = _host_layout(inputs)
    nc = build_nc()
    res = run_bass_kernel_spmd(nc, in_maps, core_ids=list(range(8)))
    outs = []
    for b in range(8):
        o = np.asarray(res.results[b]["out"], np.float32)
        o = o.transpose(1, 2, 4, 0, 3).reshape(L, DM)
        outs.append(o)
    return np.stack(outs).astype(np.float32)


# version 14
# speedup vs baseline: 1.2226x; 1.0025x over previous
import contextlib
import numpy as np
import concourse.bass as bass
import concourse.mybir as mybir
from concourse.bass_utils import run_bass_kernel_spmd

F32 = mybir.dt.float32
F32R = mybir.dt.float32r
BF16 = mybir.dt.bfloat16
AF = mybir.ActivationFunctionType
OP = mybir.AluOpType
AX = mybir.AxisListType

DM = 4096
L = 2048
NB = 32
NST = 16
TC = 256
NCH = L // TC
DK = 32

ENGS = ("sync", "tensor", "vector", "scalar", "gpsimd")
SAFE_DIST = 2


class Sched:

    def __init__(self):
        self.plan = {e: [] for e in ENGS}
        self.cnt = {e: 0 for e in ENGS}
        self.qcnt = {}
        self.last_wait = {e: {} for e in ENGS}
        self.sems = {}
        self.used = {k: set() for k in ENGS}

    def _wait(self, eng, sem, val):
        if val <= 0:
            return
        if sem == eng and (self.cnt[eng] - val) > SAFE_DIST:
            return
        lw = self.last_wait[eng].get(sem, 0)
        if val <= lw:
            return
        self.last_wait[eng][sem] = val
        if sem in self.used:
            self.used[sem].add(val)
        self.plan[eng].append(("wait", sem, val))

    def op(self, eng, fn, deps=()):
        for (s, v) in deps:
            self._wait(eng, s, v)
        self.cnt[eng] += 1
        self.plan[eng].append(("op", fn, eng, self.cnt[eng]))
        return (eng, self.cnt[eng])

    def dma(self, q, sem, fn, deps=()):
        eng = "sync" if q == "qs" else "gpsimd"
        for (s, v) in deps:
            self._wait(eng, s, v)
        self.qcnt[sem] = self.qcnt.get(sem, 0) + 1
        self.plan[eng].append(("dma", fn, sem))
        return (sem, self.qcnt[sem] * 16)

    def raw(self, eng, fn):
        self.plan[eng].append(("raw", fn))

    def finalize(self):
        self.rank = {}
        for k, s in self.used.items():
            self.rank[k] = {v: i + 1 for i, v in enumerate(sorted(s))}
        fin = [(sem, n * 16) for sem, n in self.qcnt.items()]

        def emitter(eng):
            def run(h):
                for rec in self.plan[eng]:
                    if rec[0] == "wait":
                        _, sem, v = rec
                        if sem in self.rank:
                            v = self.rank[sem][v]
                        h.wait_ge(self.sems[sem], v)
                    elif rec[0] == "dma":
                        _, fn, sem = rec
                        fn(h).then_inc(self.sems[sem], 16)
                    elif rec[0] == "op":
                        _, fn, sem, idx = rec
                        inst = fn(h)
                        if idx in self.used[sem]:
                            inst.then_inc(self.sems[sem], 1)
                if eng == "sync":
                    for (sem, v) in fin:
                        h.wait_ge(self.sems[sem], v)
            return run
        return emitter


class Slot:
    def __init__(self):
        self.wr = None
        self.rd = {}

    def write(self, tok):
        self.wr = tok
        self.rd = {}

    def read(self, tok):
        self.rd[tok[0]] = tok

    def rdeps(self):
        return [self.wr] if self.wr else []

    def wdeps(self):
        d = list(self.rd.values())
        if self.wr:
            d.append(self.wr)
        return d


def TOK(*deps):
    out = []
    for d in deps:
        if d is None:
            continue
        if isinstance(d, list):
            out.extend(x for x in d if x)
        elif d:
            out.append(d)
    return out


_DEBUG_XMID = False
_N_CHUNKS = NCH
_SKIP_B = False
_B_LIMIT = 4
_B0_LIMIT = 99
_M_LIMIT = 4


def build_nc():
    nc = bass.Bass("TRN2", target_bir_lowering=False)
    d_x = nc.dram_tensor("x", [NCH, 128, NB, TC], F32R, kind="ExternalInput").ap()
    d_xw1 = nc.dram_tensor("xw1", [8, 128, 4, 128], F32R, kind="ExternalInput").ap()
    d_xw2 = nc.dram_tensor("xw2", [128, NB * 32], F32R, kind="ExternalInput").ap()
    d_dtw = nc.dram_tensor("dtw", [8, 128, 4, 128], F32R, kind="ExternalInput").ap()
    d_alog = nc.dram_tensor("alog", [128, NB * 16], F32, kind="ExternalInput").ap()
    d_dsb = nc.dram_tensor("dsb", [128, NB], F32, kind="ExternalInput").ap()
    d_dtb = nc.dram_tensor("dtb", [128, NB], F32, kind="ExternalInput").ap()
    d_ones = nc.dram_tensor("onesr", [65, 128], F32R, kind="ExternalInput").ap()
    d_onesb = nc.dram_tensor("onesb", [33, 128], BF16, kind="ExternalInput").ap()
    d_onesc = nc.dram_tensor("onesc", [128, 1], F32R, kind="ExternalInput").ap()
    d_onesbc = nc.dram_tensor("onesbc", [128, 1], BF16, kind="ExternalInput").ap()
    d_ident = nc.dram_tensor("ident", [128, 128], F32R, kind="ExternalInput").ap()
    d_identb = nc.dram_tensor("identb", [128, 128], BF16, kind="ExternalInput").ap()
    d_seg = nc.dram_tensor("seg", [128, 8], BF16, kind="ExternalInput").ap()
    d_rep = nc.dram_tensor("rep", [8, 128], F32R, kind="ExternalInput").ap()
    d_bcol = nc.dram_tensor("bcol", [128, NB], F32, kind="ExternalInput").ap()
    d_cst = nc.dram_tensor("cst", [128, 3], F32, kind="ExternalInput").ap()
    d_wq = nc.dram_tensor("wq", [128, NB * 256], F32, kind="ExternalInput").ap()
    d_wk = nc.dram_tensor("wk", [128, NB * 256], F32, kind="ExternalInput").ap()
    d_aff = nc.dram_tensor("aff", [128, NB * 128], F32, kind="ExternalInput").ap()
    d_bm = nc.dram_tensor("bm", [128, NB * 128], F32, kind="ExternalInput").ap()
    d_wt = nc.dram_tensor("wt", [NB, 128, NB, 128], F32, kind="ExternalInput").ap()
    d_xmid = nc.dram_tensor("xmid", [NCH, 128, NB, TC], F32R,
                            kind="ExternalOutput" if _DEBUG_XMID else "Internal").ap()
    d_out = nc.dram_tensor("out", [NB, 2, 4, 128, TC], F32, kind="ExternalOutput").ap()

    es = contextlib.ExitStack()
    sb = lambda name, shape, dt: es.enter_context(nc.sbuf_tensor(name, shape, dt))

    xbuf = sb("xbuf", [128, 2 * NB * TC], F32R)
    xln = sb("xln", [128, NB * TC], F32R)
    dtb16 = sb("dtb16", [128, NB * TC], BF16)
    hblk = sb("hblk", [128, NST * TC], BF16)
    s_xw1 = sb("s_xw1", [128, 1024], F32R)
    s_dtw = sb("s_dtw", [128, 1024], F32R)
    wk2 = sb("wk2", [128, 2048], F32R)
    wk3 = sb("wk3", [128, 2048], F32R)
    pool = sb("pool", [128, 2560], F32)
    prods = sb("prods", [128, 512], F32R)
    wqx = sb("wqx", [128, 4096], BF16)
    bbc = sb("bbc", [128, NST * TC], BF16)
    cbc = sb("cbc", [128, NST * TC], BF16)
    xp2f = sb("xp2f", [33, 16 * TC], BF16)
    s_xw2 = sb("s_xw2", [128, NB * 32], F32R)
    s_alog = sb("s_alog", [128, NB * 16], F32)
    hstate = sb("hstate", [128, NB * NST], F32)
    xp1s = sb("xp1s", [128, TC], F32R)
    xp2s = sb("xp2s", [32, TC], BF16)
    sqst = sb("sqst", [128, 2 * TC], BF16)
    s_D = sb("s_D", [128, NB], F32)
    s_dtb = sb("s_dtb", [128, NB], F32)
    s_bcol = sb("s_bcol", [128, NB], F32)
    s_cst = sb("s_cst", [128, 3], F32)
    s_ones = sb("s_ones", [65, 128], F32R)
    s_onesb = sb("s_onesb", [33, 128], BF16)
    s_onesc = sb("s_onesc", [128, 1], F32R)
    s_onesbc = sb("s_onesbc", [128, 1], BF16)
    s_ident = sb("s_ident", [128, 128], F32R)
    s_identb = sb("s_identb", [128, 128], BF16)
    s_seg = sb("s_seg", [128, 8], BF16)
    s_rep = sb("s_rep", [8, 128], F32R)
    rows_t = sb("rows_t", [1, 2 * TC], F32)
    rows_r = sb("rows_r", [1, 2 * TC], F32)
    scw = sb("scw", [68, TC], F32)
    wtst = sb("wtst", [8, TC], F32R)
    r_mx = sb("r_mx", [128, 1], F32)
    r_bias = sb("r_bias", [128, 1], F32)
    r_ex = sb("r_ex", [128, 8], F32)
    r_sum = sb("r_sum", [128, 1], F32)
    r_rec = sb("r_rec", [128, 1], F32)
    r_wsm = sb("r_wsm", [128, 8], F32)
    r_mean = rows_t[0:1, 0:TC]
    r_var = rows_t[0:1, TC:2 * TC]
    r_rstd = rows_r[0:1, 0:TC].bitcast(F32R)
    r_lnv = rows_r[0:1, TC:2 * TC].bitcast(F32R)
    r_mr = rows_r[0:1, TC:2 * TC].bitcast(F32R)
    sc1s = scw[0:4, :]
    sc2s = scw[64:68, :]
    wts = wtst[:]

    def xv_of(par):
        return xbuf[:, par * NB * TC:(par + 1) * NB * TC].rearrange(
            "p (b t) -> p b t", b=NB)
    xlnv = xln[:].rearrange("p (b t) -> p b t", b=NB)
    xw1v = s_xw1[:].rearrange("p (x b t) -> p x b t", x=2, b=4)
    dtwv = s_dtw[:].rearrange("p (x b t) -> p x b t", x=2, b=4)
    hblk_last = hblk[:].rearrange("p (n t) -> p n t", n=NST)[:, :, TC - 1]
    NPAR = 3
    a_scr = [pool[:, i * 256:(i + 1) * 256] for i in range(NPAR)]
    b_grp = [pool[:, 768:1792]] * 2
    prodb = prods[:].bitcast(BF16)
    e1_scr = [pool[:, 1792:2048]] * 2
    sqs = sqst[:]
    w_scr = [pool[:, 2048:2304]] * 2
    t1_scr = [pool[:, 2304:2560]] * 2
    res_v = [pool[:, i * 256:(i + 1) * 256] for i in range(2)]
    t1B = [pool[:, i * 256:(i + 1) * 256] for i in range(2)]
    outs_v = [pool[:, 512 + i * 256:512 + (i + 1) * 256] for i in range(2)]
    q_v = [pool[:, 1024 + i * 256:1024 + (i + 1) * 256] for i in range(2)]
    qk_v = [pool[:, 1536 + i * 128:1536 + (i + 1) * 128].bitcast(BF16)
            for i in range(2)]
    h_v = pool[:, 1792:2048]
    wh_v = pool[:, 2048:2560].bitcast(BF16)
    xln2h = xbuf[:].bitcast(BF16).rearrange("p (b t) -> p b t", b=NB)
    xmid_ch = xlnv
    wsl = dtb16[:].rearrange("p (w b o) -> p w b o", w=2, b=NB)
    wq_lo = hblk[:].rearrange("p (b t) -> p b t", b=16)
    wq_hi = wqx[:].rearrange("p (b t) -> p b t", b=16)
    wkA = wk2[:].bitcast(BF16).rearrange("p (b t) -> p b t", b=16)
    wkB = wk3[:].bitcast(BF16).rearrange("p (b t) -> p b t", b=16)
    affv = bbc[:].rearrange("p (b t) -> p b t", b=NB)
    bmv = cbc[:].rearrange("p (b t) -> p b t", b=NB)

    pb = [es.enter_context(nc.psum_tensor(f"pb{i}", [128, 512], F32)) for i in range(8)]

    S = Sched()
    sl_x = [Slot(), Slot()]
    sl_xln = Slot()
    sl_dt = Slot()
    sl_bc = Slot()
    sl_sq = [Slot(), Slot()]
    sl_a = [Slot() for _ in range(4)]
    sl_b = [Slot() for _ in range(4)]
    sl_prod = [Slot() for _ in range(4)]
    sl_pb = [Slot() for _ in range(8)]
    sl_xp1 = Slot()
    sl_xp2f = Slot()
    sl_rows = Slot()
    sl_wscr = [Slot(), Slot()]
    sl_e1 = [Slot(), Slot()]
    sl_t1 = [Slot(), Slot()]
    sl_xw1 = [Slot(), Slot()]
    sl_dtw = [Slot(), Slot()]

    t_w = {}
    for nm, dst, src in [
        ("xw2", s_xw2, d_xw2), ("alog", s_alog, d_alog), ("dsb", s_D, d_dsb),
        ("dtb", s_dtb, d_dtb), ("bcol", s_bcol, d_bcol),
        ("ones", s_ones, d_ones), ("onesb", s_onesb, d_onesb),
        ("ident", s_ident, d_ident), ("seg", s_seg, d_seg), ("rep", s_rep, d_rep),
        ("onesc", s_onesc, d_onesc), ("onesbc", s_onesbc, d_onesbc),
        ("cst", s_cst, d_cst), ("identb", s_identb, d_identb),
    ]:
        t_w[nm] = S.dma("qs", "w0", lambda h, dst=dst, src=src: h.dma_start(out=dst[:], in_=src))
    t_w = {k: ("w0", S.qcnt["w0"] * 16) for k in t_w}
    for c in range(2):
        tok = S.dma("qs", f"xb{c % 2}", lambda h, c=c: h.dma_start(
            out=xv_of(c % 2), in_=d_x[c]))
        sl_x[c % 2].write(tok)
    for qq in range(2):
        tok = S.dma("qs", f"xw1{qq}", lambda h, qq=qq: h.dma_start(out=xw1v[:, qq], in_=d_xw1[qq]))
        sl_xw1[qq].write(tok)
        tok = S.dma("qs", f"dtw{qq}", lambda h, qq=qq: h.dma_start(out=dtwv[:, qq], in_=d_dtw[qq]))
        sl_dtw[qq].write(tok)
    ta = S.op("scalar", lambda h: h.activation(pool[:, 0:512], s_alog[:], AF.Exp,
                                               scale=1.0, bias=0.0),
              deps=[t_w["alog"]])
    t_A = S.op("vector", lambda h: h.tensor_scalar_mul(s_alog[:], pool[:, 0:512], -1.0),
               deps=[ta])
    sl_a[0].write(t_A)
    sl_a[1].write(t_A)
    t_hs0 = S.op("vector", lambda h: h.memset(hstate[:], 0.0))

    stats_tok = {}
    stats_mm_tok = {}

    def emit_stats(c):
        par = c % 2
        xv = xv_of(par)
        for k in range(NB):
            tsq = S.op("scalar", lambda h, k=k, xv=xv: h.activation(
                sqs[:, (k % 2) * TC:(k % 2 + 1) * TC], xv[:, k].bitcast(F32),
                AF.Square, scale=1.0, bias=0.0),
                deps=TOK(sl_x[par].rdeps(), sl_sq[k % 2].wdeps()))
            sl_sq[k % 2].write(tsq)
            sl_x[par].read(tsq)
            tmm = S.op("tensor", lambda h, k=k, xv=xv: h.matmul(
                pb[0][0:1, 0:TC], s_onesc[:].bitcast(F32), xv[:, k].bitcast(F32),
                start=(k == 0), stop=(k == NB - 1)),
                deps=TOK([t_w["onesc"]], sl_x[par].rdeps(),
                         sl_pb[0].wdeps() if k == 0 else []))
            sl_x[par].read(tmm)
            if k == 0:
                sl_pb[0].write(tmm)
            tm2 = S.op("tensor", lambda h, k=k: h.matmul(
                pb[1][0:1, 0:TC], s_onesbc[:], sqs[:, (k % 2) * TC:(k % 2 + 1) * TC],
                start=(k == 0), stop=(k == NB - 1)),
                deps=TOK([t_w["onesbc"], tsq], sl_pb[1].wdeps() if k == 0 else []))
            sl_sq[k % 2].read(tm2)
            if k == 0:
                sl_pb[1].write(tm2)
            t_sx_done, t_sxx_done = tmm, tm2
        stats_mm_tok[c] = (t_sx_done, t_sxx_done)

    def emit_stats_fin(c):
        t_sx_done, t_sxx_done = stats_mm_tok[c]
        td = S.op("vector", lambda h: h.tensor_scalar_mul(
            r_mean, pb[0][0:1, 0:TC], 1.0 / DM),
            deps=TOK([t_sx_done], sl_rows.wdeps()))
        sl_rows.write(td)
        sl_pb[0].read(td)
        td2 = S.op("vector", lambda h: h.tensor_mul(r_var, r_mean, r_mean),
                   deps=[td])
        td3 = S.op("vector", lambda h: h.scalar_tensor_tensor(
            r_var, pb[1][0:1, 0:TC], 1.0 / DM, r_var, OP.mult, OP.subtract),
            deps=TOK([t_sxx_done, td2]))
        sl_pb[1].read(td3)
        td4 = S.op("scalar", lambda h: h.activation(r_lnv, r_var, AF.Ln,
                                                    scale=1.0, bias=s_cst[0:1, 0:1]), deps=TOK([td3, t_w['cst']]))
        td5 = S.op("scalar", lambda h: h.activation(r_rstd, r_lnv.bitcast(F32), AF.Exp,
                                                    scale=s_cst[0:1, 1:2], bias=0.0), deps=[td4])
        td6 = S.op("vector", lambda h: h.tensor_mul(r_mr, r_mean,
                                                    r_rstd.bitcast(F32)), deps=[td5])
        for _t in (td4, td5, td6):
            sl_rows.read(_t)
        tb1 = S.op("tensor", lambda h: h.matmul(pb[0][:, 0:TC], s_ones[0:1, :], r_rstd,
                                                start=True, stop=True),
                   deps=TOK([td5], sl_pb[0].wdeps()))
        sl_pb[0].write(tb1)
        sl_rows.read(tb1)
        tb2 = S.op("tensor", lambda h: h.matmul(pb[1][:, 0:TC], s_ones[0:1, :], r_mr,
                                                start=True, stop=True),
                   deps=TOK([td6], sl_pb[1].wdeps()))
        sl_pb[1].write(tb2)
        sl_rows.read(tb2)
        stats_tok[c] = (tb1, tb2)

    for c in range(_N_CHUNKS):
        par = c % 2
        xv = xv_of(par)
        if c == 0:
            emit_stats(0)
            emit_stats_fin(0)
        tb1, tb2 = stats_tok[c]
        t_xln = []
        for k in range(NB):
            tt1 = S.op("vector", lambda h, k=k, xv=xv: h.tensor_mul(
                t1_scr[k % 2], xv[:, k].bitcast(F32), pb[0][:, 0:TC]),
                deps=TOK([tb1], sl_t1[k % 2].wdeps(), sl_x[par].rdeps()))
            sl_x[par].read(tt1)
            sl_pb[0].read(tt1)
            tt2 = S.op("vector", lambda h, k=k: h.tensor_tensor(
                xlnv[:, k], t1_scr[k % 2], pb[1][:, 0:TC], OP.subtract),
                deps=TOK([tb2], sl_xln.wdeps() if k == 0 else []))
            sl_t1[k % 2].write(tt2)
            sl_pb[1].read(tt2)
            if k == 0:
                sl_xln.write(tt2)
            t_xln.append(tt2)
            tx2 = S.op("vector", lambda h, k=k, xv=xv: h.scalar_tensor_tensor(
                xv[:, k], xlnv[:, k].bitcast(F32), s_D[:, k:k + 1],
                xv[:, k].bitcast(F32), OP.mult, OP.add),
                deps=TOK([t_w["dsb"], tt2]))
            sl_x[par].read(tx2)
            sl_xln.read(tx2)
        for k in range(NB):
            qq = k // 4
            t = S.op("tensor", lambda h, k=k, qq=qq: h.matmul(
                pb[2][:, 0:TC], xw1v[:, qq % 2, k % 4], xlnv[:, k],
                start=(k == 0), stop=(k == NB - 1)),
                deps=TOK(sl_xw1[qq % 2].rdeps(), [t_xln[k]],
                         sl_pb[2].wdeps() if k == 0 else []))
            sl_xln.read(t)
            sl_xw1[qq % 2].read(t)
            if k == 0:
                sl_pb[2].write(t)
            t_xp1 = t
            if k % 4 == 3 and (c < NCH - 1 or qq < 7):
                nq = (qq + 1) % 8
                tok = S.dma("qs", f"xw1{nq % 2}", lambda h, nq=nq: h.dma_start(
                    out=xw1v[:, nq % 2], in_=d_xw1[nq]),
                    deps=TOK(sl_xw1[nq % 2].wdeps()))
                sl_xw1[nq % 2].write(tok)
        for k in range(NB):
            t = S.op("tensor", lambda h, k=k: h.matmul(
                pb[3][0:32, 0:TC], s_xw2[:, k * 32:(k + 1) * 32], xlnv[:, k],
                start=(k == 0), stop=(k == NB - 1)),
                deps=TOK([t_w["xw2"], t_xln[k]], sl_pb[3].wdeps() if k == 0 else []))
            sl_xln.read(t)
            if k == 0:
                sl_pb[3].write(t)
            t_xp2 = t
        tev1 = S.op("scalar", lambda h: h.activation(xp1s[:], pb[2][:, 0:TC], AF.Copy,
                                                     scale=1.0, bias=0.0),
                    deps=TOK([t_xp1], sl_xp1.wdeps()))
        sl_pb[2].read(tev1)
        sl_xp1.write(tev1)
        tev2 = S.op("scalar", lambda h: h.activation(xp2s[:], pb[3][0:32, 0:TC], AF.Copy,
                                                     scale=1.0, bias=0.0),
                    deps=TOK([t_xp2], sl_xp2f.wdeps()))
        sl_pb[3].read(tev2)
        tfl = S.dma("qg", "xpf", lambda h: h.dma_start(out=xp2f[0:1, :], in_=xp2s[0:16, :]),
                    deps=TOK([tev2]))
        tfl = S.dma("qg", "xpf", lambda h: h.dma_start(out=xp2f[32:33, :], in_=xp2s[16:32, :]),
                    deps=TOK([tev2]))
        sl_xp2f.write(tfl)
        t_dt = []
        for k in range(NB):
            qq = k // 4
            tz = S.op("tensor", lambda h, k=k, qq=qq: h.matmul(
                pb[4 + k % 2][:, 0:TC], dtwv[:, qq % 2, k % 4], xp1s[:],
                start=True, stop=True),
                deps=TOK(sl_dtw[qq % 2].rdeps(), [tev1], sl_pb[4 + k % 2].wdeps()))
            sl_pb[4 + k % 2].write(tz)
            sl_xp1.read(tz)
            sl_dtw[qq % 2].read(tz)
            te = S.op("scalar", lambda h, k=k: h.activation(
                e1_scr[k % 2], pb[4 + k % 2][:, 0:TC], AF.Exp,
                scale=1.0, bias=s_dtb[:, k:k + 1]),
                deps=TOK([t_w["dtb"], tz], sl_e1[k % 2].wdeps()))
            sl_pb[4 + k % 2].read(te)
            sl_e1[k % 2].write(te)
            tl = S.op("scalar", lambda h, k=k: h.activation(
                dtb16[:, k * TC:(k + 1) * TC], e1_scr[k % 2], AF.Ln,
                scale=1.0, bias=1.0),
                deps=TOK([te], sl_dt.wdeps() if k == 0 else []))
            sl_e1[k % 2].read(tl)
            if k == 0:
                sl_dt.write(tl)
            t_dt.append(tl)
            if k % 4 == 3 and (c < NCH - 1 or qq < 7):
                nq = (qq + 1) % 8
                tok = S.dma("qs", f"dtw{nq % 2}", lambda h, nq=nq: h.dma_start(
                    out=dtwv[:, nq % 2], in_=d_dtw[nq]),
                    deps=TOK(sl_dtw[nq % 2].wdeps()))
                sl_dtw[nq % 2].write(tok)
        t_bcB = [None] * 8
        t_bcC = [None] * 8
        for j in range(16):
            isB = j < 8
            tgt = bbc if isB else cbc
            jn = j % 8
            row = xp2f[0:1, :] if isB else xp2f[32:33, :]
            lhs1 = s_onesb[0:1, :] if isB else s_onesb[32:33, :]
            tb = S.op("tensor", lambda h, jn=jn, row=row, lhs1=lhs1, j=j: h.matmul(
                pb[2 + j % 2][:, 0:2 * TC], lhs1, row[:, jn * 2 * TC:(jn + 1) * 2 * TC],
                start=True, stop=True),
                deps=TOK([tfl, t_w["onesb"]], sl_pb[2 + j % 2].wdeps()))
            sl_pb[2 + j % 2].write(tb)
            sl_xp2f.read(tb)
            tv = S.op("scalar", lambda h, tgt=tgt, jn=jn, j=j: h.activation(
                tgt[:, jn * 2 * TC:(jn + 1) * 2 * TC], pb[2 + j % 2][:, 0:2 * TC],
                AF.Copy, scale=1.0, bias=0.0),
                deps=TOK([tb], sl_bc.wdeps() if j == 0 else []))
            sl_pb[2 + j % 2].read(tv)
            if j == 0:
                sl_bc.write(tv)
            if isB:
                t_bcB[jn] = tv
            else:
                t_bcC[jn] = tv
        t_bcast = t_bcC[7]
        pend_ty = []
        for k in range(NB):
            kp = k % 2
            tw = S.op("gpsimd", lambda h, k=k, kp=kp: h.tensor_tensor(
                w_scr[kp], dtb16[:, k * TC:(k + 1) * TC], xlnv[:, k].bitcast(F32),
                OP.mult),
                deps=TOK([t_dt[k]], sl_wscr[kp].wdeps()))
            sl_dt.read(tw)
            sl_xln.read(tw)
            sl_wscr[kp].write(tw)
            t_aexp = [None] * NST
            t_bb = [None] * 4
            t_scan = [None] * NST
            t_cm = [None] * 4
            t_id = [None] * NST

            def emit_a(n, k=k, t_aexp=t_aexp):
                p4 = n % NPAR
                t = S.op("scalar", lambda h, k=k, n=n, p4=p4: h.activation(
                    a_scr[p4], dtb16[:, k * TC:(k + 1) * TC], AF.Exp,
                    scale=s_alog[:, k * NST + n:k * NST + n + 1], bias=0.0),
                    deps=TOK([t_A, t_dt[k]], sl_a[p4].wdeps()))
                sl_a[p4].write(t)
                sl_dt.read(t)
                t_aexp[n] = t

            def emit_bgrp(g, k=k, kp=kp, t_bb=t_bb, tw=tw):
                gp = g % 2
                t = S.op("vector", lambda h, g=g, gp=gp, kp=kp: h.tensor_tensor(
                    b_grp[gp].rearrange("p (x t) -> p x t", x=4),
                    w_scr[kp].unsqueeze(1).broadcast_to((128, 4, TC)),
                    bbc[:, g * 4 * TC:(g + 1) * 4 * TC].rearrange(
                        "p (x t) -> p x t", x=4), OP.mult),
                    deps=TOK([tw, t_bcB[g * 2 + 1]], sl_b[gp].wdeps()))
                sl_wscr[kp].read(t)
                sl_bc.read(t)
                sl_b[gp].write(t)
                t_bb[g] = t

            def emit_scan(n, k=k, t_scan=t_scan, t_aexp=t_aexp, t_bb=t_bb):
                p4 = n % NPAR
                g, gp = n // 4, (n // 4) % 2
                t = S.op("vector", lambda h, k=k, n=n, p4=p4, gp=gp: h.tensor_tensor_scan(
                    hblk[:, n * TC:(n + 1) * TC], a_scr[p4],
                    b_grp[gp][:, (n % 4) * TC:(n % 4 + 1) * TC],
                    hstate[:, k * NST + n:k * NST + n + 1], OP.mult, OP.add),
                    deps=TOK([t_aexp[n], t_bb[g], t_hs0]))
                sl_a[p4].read(t)
                sl_b[gp].read(t)
                t_scan[n] = t

            def emit_cmgrp(g, t_cm=t_cm, t_scan=t_scan):
                t = S.op("vector", lambda h, g=g: h.tensor_tensor(
                    prodb, hblk[:, g * 4 * TC:(g + 1) * 4 * TC],
                    cbc[:, g * 4 * TC:(g + 1) * 4 * TC], OP.mult),
                    deps=TOK([t_scan[g * 4 + 3], t_bcC[g * 2 + 1]], sl_prod[0].wdeps()))
                sl_prod[0].write(t)
                sl_bc.read(t)
                t_cm[g] = t

            def emit_id(n, k=k, t_id=t_id, t_cm=t_cm):
                t = S.op("tensor", lambda h, k=k, n=n: h.matmul(
                    pb[6 + k % 2][:, 0:TC], s_identb[:],
                    prodb[:, (n % 4) * TC:(n % 4 + 1) * TC],
                    start=(n == 0), stop=(n == NST - 1)),
                    deps=TOK([t_w["identb"], t_cm[n // 4]],
                             sl_pb[6 + k % 2].wdeps() if n == 0 else []))
                sl_prod[0].read(t)
                if n == 0:
                    sl_pb[6 + k % 2].write(t)
                t_id[n] = t

            emit_a(0)
            emit_bgrp(0)
            emit_a(1)
            for n in range(NST):
                emit_scan(n)
                if n + 2 < NST:
                    emit_a(n + 2)
                if n % 4 == 3 and n // 4 + 1 < 4:
                    emit_bgrp(n // 4 + 1)
                if n % 4 == 0 and n >= 4:
                    g = n // 4 - 1
                    emit_cmgrp(g)
                    for j in range(4):
                        emit_id(g * 4 + j)
            emit_cmgrp(3)
            for j in range(4):
                emit_id(12 + j)
            S.op("gpsimd", lambda h, k=k: h.tensor_copy(
                hstate[:, k * NST:(k + 1) * NST], hblk_last),
                deps=TOK([t_scan[NST - 1]]))
            pend_ty.append((k, t_id[NST - 1], xv))
            if len(pend_ty) > 1 or k == NB - 1:
                for (k2, tok_id, xv2) in (pend_ty[:-1] if k < NB - 1 else pend_ty):
                    ty = S.op("vector", lambda h, k2=k2, xv2=xv2: h.tensor_tensor(
                        xv2[:, k2], xv2[:, k2].bitcast(F32),
                        pb[6 + k2 % 2][:, 0:TC], OP.add),
                        deps=TOK([tok_id]))
                    sl_pb[6 + k2 % 2].read(ty)
                    sl_x[par].read(ty)
                pend_ty[:] = [] if k == NB - 1 else pend_ty[-1:]
            if k == 12 and c + 1 < NCH:
                emit_stats(c + 1)
            if k == 26 and c + 1 < NCH:
                emit_stats_fin(c + 1)
        tst = S.dma("qg", f"xms{par}", lambda h, c=c, xv=xv: h.dma_start(out=d_xmid[c], in_=xv),
                    deps=TOK([sl_x[par].rd.get("vector")]))
        sl_x[par].read(tst)
        if c + 2 < NCH:
            tok = S.dma("qs", f"xb{par}", lambda h, c=c: h.dma_start(
                out=xv_of(c % 2), in_=d_x[c + 2]), deps=TOK(sl_x[par].wdeps()))
            sl_x[par].write(tok)

    allbar = [(e, S.cnt[e]) for e in ENGS if S.cnt[e] > 0]
    allbar += [(sem, n * 16) for sem, n in S.qcnt.items()]

    t_wq = S.dma("qg", "wB", lambda h: h.dma_start(out=hblk[:], in_=d_wq[:, 0:16 * 256]),
                 deps=allbar)
    t_wq2 = S.dma("qg", "wB", lambda h: h.dma_start(out=wqx[:], in_=d_wq[:, 16 * 256:NB * 256]),
                  deps=allbar)
    t_wkA = S.dma("qg", "wB", lambda h: h.dma_start(out=wk2[:].bitcast(BF16),
                                                    in_=d_wk[:, 0:16 * 256]), deps=allbar)
    t_wkB = S.dma("qg", "wB", lambda h: h.dma_start(out=wk3[:].bitcast(BF16),
                                                    in_=d_wk[:, 16 * 256:32 * 256]),
                  deps=allbar)
    t_aff = S.dma("qg", "wB", lambda h: h.dma_start(out=bbc[:], in_=d_aff),
                  deps=allbar)
    t_bm = S.dma("qg", "wB", lambda h: h.dma_start(out=cbc[:], in_=d_bm),
                 deps=allbar)
    t_wq = t_wkA = t_wkB = t_aff = t_bm = ("wB", S.qcnt["wB"] * 16)
    sl_xl2 = Slot()
    sl_wh = Slot()
    sl_wsl = [Slot(), Slot()]
    sl_res = [Slot(), Slot()]
    sl_outs = [Slot(), Slot()]
    sl_xmc = Slot()
    sl_qv = [Slot(), Slot()]
    sl_qkv = [Slot(), Slot()]
    sl_hv = Slot()
    sl_sc = Slot()
    sl_wts = Slot()

    prev_soft = {"tex": None, "twm": None, "ttb": None}
    for half in range(0 if _SKIP_B else (2 if _B_LIMIT >= 4 else 1)):
        for cc in range(4 if _B_LIMIT >= 2 else 0):
            g = half * 4 + cc
            tld = S.dma("qs", "xmc", lambda h, g=g: h.dma_start(out=xmid_ch, in_=d_xmid[g]),
                        deps=TOK(allbar if (half == 0 and cc == 0) else [],
                                 sl_xmc.wdeps()))
            sl_xmc.write(tld)
            for k in range(NB):
                tsq = S.op("scalar", lambda h, k=k: h.activation(
                    sqs[:, (k % 2) * TC:(k % 2 + 1) * TC], xmid_ch[:, k].bitcast(F32),
                    AF.Square, scale=1.0, bias=0.0),
                    deps=TOK([tld], sl_sq[k % 2].wdeps(),
                             allbar if (half == 0 and cc == 0 and k == 0) else []))
                sl_sq[k % 2].write(tsq)
                sl_xmc.read(tsq)
                tmm = S.op("tensor", lambda h, k=k: h.matmul(
                    pb[0][0:1, 0:TC], s_onesc[:], xmid_ch[:, k],
                    start=(k == 0), stop=(k == NB - 1)),
                    deps=TOK([tld], sl_pb[0].wdeps() if k == 0 else [],
                             allbar if (half == 0 and cc == 0 and k == 0) else []))
                sl_xmc.read(tmm)
                if k == 0:
                    sl_pb[0].write(tmm)
                tm2 = S.op("tensor", lambda h, k=k: h.matmul(
                    pb[1][0:1, 0:TC], s_onesbc[:], sqs[:, (k % 2) * TC:(k % 2 + 1) * TC],
                    start=(k == 0), stop=(k == NB - 1)),
                    deps=TOK([tsq], sl_pb[1].wdeps() if k == 0 else []))
                sl_sq[k % 2].read(tm2)
                if k == 0:
                    sl_pb[1].write(tm2)
                t_sx_done, t_sxx_done = tmm, tm2
            td = S.op("vector", lambda h: h.tensor_scalar_mul(
                r_mean, pb[0][0:1, 0:TC], 1.0 / DM),
                deps=TOK([t_sx_done], sl_rows.wdeps(),
                         allbar if (half == 0 and cc == 0) else []))
            sl_rows.write(td)
            sl_pb[0].read(td)
            td2 = S.op("vector", lambda h: h.tensor_mul(r_var, r_mean, r_mean),
                       deps=[td])
            td3 = S.op("vector", lambda h: h.scalar_tensor_tensor(
                r_var, pb[1][0:1, 0:TC], 1.0 / DM, r_var, OP.mult, OP.subtract),
                deps=TOK([t_sxx_done, td2]))
            sl_pb[1].read(td3)
            td4 = S.op("scalar", lambda h: h.activation(r_lnv, r_var, AF.Ln,
                                                        scale=1.0, bias=s_cst[0:1, 0:1]),
                       deps=TOK([td3, t_w['cst']]))
            td5 = S.op("scalar", lambda h: h.activation(r_rstd, r_lnv.bitcast(F32), AF.Exp,
                                                        scale=s_cst[0:1, 1:2], bias=0.0),
                       deps=[td4])
            td6 = S.op("vector", lambda h: h.tensor_mul(
                r_mr, r_mean, r_rstd.bitcast(F32)), deps=[td5])
            for _t in (td4, td5, td6):
                sl_rows.read(_t)
            tb1 = S.op("tensor", lambda h: h.matmul(pb[0][:, 0:TC], s_ones[0:1, :],
                                                    r_rstd, start=True, stop=True),
                       deps=TOK([td5], sl_pb[0].wdeps()))
            sl_pb[0].write(tb1)
            sl_rows.read(tb1)
            tb2 = S.op("tensor", lambda h: h.matmul(pb[1][:, 0:TC], s_ones[0:1, :], r_mr,
                                                    start=True, stop=True),
                       deps=TOK([td6], sl_pb[1].wdeps()))
            sl_pb[1].write(tb2)
            sl_rows.read(tb2)
            if _B0_LIMIT < 2:
                continue
            t_x2 = []
            for k in range(NB):
                tt1 = S.op("vector", lambda h, k=k: h.tensor_mul(
                    t1B[k % 2], xmid_ch[:, k].bitcast(F32), pb[0][:, 0:TC]),
                    deps=TOK([tb1], sl_t1[k % 2].wdeps(), sl_res[k % 2].wdeps()))
                sl_xmc.read(tt1)
                sl_pb[0].read(tt1)
                sl_res[k % 2].read(tt1)
                tt2 = S.op("vector", lambda h, k=k, cc=cc: h.tensor_tensor(
                    xln2h[:, k, cc * TC:(cc + 1) * TC], t1B[k % 2],
                    pb[1][:, 0:TC], OP.subtract),
                    deps=TOK([tb2], sl_xl2.wdeps() if (k == 0 and cc == 0) else []))
                sl_t1[k % 2].write(tt2)
                sl_res[k % 2].read(tt2)
                sl_pb[1].read(tt2)
                if k == 0 and cc == 0:
                    sl_xl2.write(tt2)
                t_x2.append(tt2)
            if _B0_LIMIT < 3:
                continue
            for k in range(NB):
                wq_src = wq_lo if k < 16 else wq_hi
                tq1 = S.op("tensor", lambda h, k=k, cc=cc, wq_src=wq_src: h.matmul(
                    pb[2][:, 0:TC], wq_src[:, k % 16, 0:128],
                    xln2h[:, k, cc * TC:(cc + 1) * TC],
                    start=(k == 0), stop=(k == NB - 1)),
                    deps=TOK([t_wq, t_x2[k]], sl_pb[2].wdeps() if k == 0 else []))
                sl_xl2.read(tq1)
                if k == 0:
                    sl_pb[2].write(tq1)
                tq2 = S.op("tensor", lambda h, k=k, cc=cc, wq_src=wq_src: h.matmul(
                    pb[3][:, 0:TC], wq_src[:, k % 16, 128:256],
                    xln2h[:, k, cc * TC:(cc + 1) * TC],
                    start=(k == 0), stop=(k == NB - 1)),
                    deps=TOK([t_wq, t_x2[k]], sl_pb[3].wdeps() if k == 0 else []))
                if k == 0:
                    sl_pb[3].write(tq2)
                wk_src = wkA if k < 16 else wkB
                tk1 = S.op("tensor", lambda h, k=k, cc=cc, wk_src=wk_src: h.matmul(
                    pb[4][:, 0:TC], wk_src[:, k % 16, 0:128],
                    xln2h[:, k, cc * TC:(cc + 1) * TC],
                    start=(k == 0), stop=(k == NB - 1)),
                    deps=TOK([t_wkA, t_wkB, t_x2[k]],
                             sl_pb[4].wdeps() if k == 0 else []))
                if k == 0:
                    sl_pb[4].write(tk1)
                tk2 = S.op("tensor", lambda h, k=k, cc=cc, wk_src=wk_src: h.matmul(
                    pb[5][:, 0:TC], wk_src[:, k % 16, 128:256],
                    xln2h[:, k, cc * TC:(cc + 1) * TC],
                    start=(k == 0), stop=(k == NB - 1)),
                    deps=TOK([t_wkA, t_wkB, t_x2[k]],
                             sl_pb[5].wdeps() if k == 0 else []))
                if k == 0:
                    sl_pb[5].write(tk2)
                th = S.op("tensor", lambda h, k=k, cc=cc: h.matmul(
                    pb[6][:, 0:TC], affv[:, k], xln2h[:, k, cc * TC:(cc + 1) * TC],
                    start=(k == 0), stop=(k == NB - 1)),
                    deps=TOK([t_aff, t_x2[k]], sl_pb[6].wdeps() if k == 0 else []))
                if k == 0:
                    sl_pb[6].write(th)
                t_qd, t_q2d, t_kd, t_k2d, t_hd = tq1, tq2, tk1, tk2, th
            if _B0_LIMIT < 4:
                continue
            t_sc_ev = []
            for g2 in range(2):
                tevq = S.op("scalar", lambda h, g2=g2: h.activation(
                    q_v[g2], pb[2 + g2][:, 0:TC], AF.Copy, scale=1.0, bias=0.0),
                    deps=TOK([t_qd if g2 == 0 else t_q2d], sl_qv[g2].wdeps()))
                sl_pb[2 + g2].read(tevq)
                sl_qv[g2].write(tevq)
                tqk = S.op("vector", lambda h, g2=g2: h.tensor_tensor(
                    qk_v[g2], pb[4 + g2][:, 0:TC], q_v[g2], OP.mult),
                    deps=TOK([tevq, t_kd if g2 == 0 else t_k2d], sl_qkv[g2].wdeps()))
                sl_pb[4 + g2].read(tqk)
                sl_qv[g2].read(tqk)
                sl_qkv[g2].write(tqk)
                bank = pb[7] if g2 == 0 else pb[0]
                base = 0 if g2 == 0 else 64
                bslot = sl_pb[7] if g2 == 0 else sl_pb[0]
                tsc_mm = S.op("tensor", lambda h, g2=g2, base=base, bank=bank: h.matmul(
                    bank[base:base + 4, 0:TC], s_seg[:, g2 * 4:(g2 + 1) * 4],
                    qk_v[g2], start=True, stop=True),
                    deps=TOK([t_w["seg"], tqk], bslot.wdeps()))
                sl_qkv[g2].read(tsc_mm)
                bslot.write(tsc_mm)
                tevs = S.op("scalar", lambda h, g2=g2, base=base, bank=bank: h.activation(
                    (sc1s if g2 == 0 else sc2s)[:], bank[base:base + 4, 0:TC],
                    AF.Copy, scale=1.0, bias=0.0),
                    deps=TOK([tsc_mm], sl_sc.wdeps() if g2 == 0 else []))
                bslot.read(tevs)
                if g2 == 0:
                    sl_sc.write(tevs)
                t_sc_ev.append(tevs)
            if _B0_LIMIT < 5:
                continue
            for ts in range(2):
                ttr1 = S.op("tensor", lambda h, ts=ts: h.transpose(
                    pb[1][:, ts * 8:ts * 8 + 4], sc1s[:, ts * 128:(ts + 1) * 128],
                    s_ident[0:4, 0:4].bitcast(F32)),
                    deps=TOK(t_sc_ev, sl_pb[1].wdeps()))
                sl_pb[1].write(ttr1)
                sl_sc.read(ttr1)
                ttr2 = S.op("tensor", lambda h, ts=ts: h.transpose(
                    pb[1][:, ts * 8 + 4:ts * 8 + 8], sc2s[:, ts * 128:(ts + 1) * 128],
                    s_ident[64:68, 64:68].bitcast(F32)),
                    deps=TOK(t_sc_ev))
                sl_sc.read(ttr2)
                if _B0_LIMIT < 6:
                    continue
                tmx = S.op("vector", lambda h, ts=ts: h.reduce_max(
                    r_mx[:], pb[1][:, ts * 8:ts * 8 + 8], axis=AX.X),
                    deps=TOK([ttr2, prev_soft["tex"]]))
                sl_pb[1].read(tmx)
                tbs = S.op("vector", lambda h: h.tensor_scalar_mul(
                    r_bias[:], r_mx[:], -1.0 / float(np.sqrt(DK))),
                    deps=TOK([tmx, prev_soft["tex"]]))
                if _B0_LIMIT < 7:
                    continue
                tex = S.op("scalar", lambda h, ts=ts: h.activation(
                    r_ex[:], pb[1][:, ts * 8:ts * 8 + 8], AF.Exp,
                    scale=s_cst[:, 2:3], bias=r_bias[:]),
                    deps=TOK([tbs, prev_soft["twm"]]))
                sl_pb[1].read(tex)
                if _B0_LIMIT < 8:
                    continue
                tsm = S.op("vector", lambda h: h.reduce_sum(
                    r_sum[:], r_ex[:], axis=AX.X), deps=TOK([tex]))
                trc = S.op("vector", lambda h: h.reciprocal(r_rec[:], r_sum[:]),
                           deps=[tsm])
                twm = S.op("vector", lambda h: h.tensor_scalar(
                    r_wsm[:].bitcast(BF16)[:, 0:8], r_ex[:], r_rec[:], None, OP.mult),
                    deps=TOK([trc, tex, prev_soft["ttb"]]))
                if _B0_LIMIT < 9:
                    continue
                ttb = S.op("tensor", lambda h: h.transpose(
                    pb[0][:].bitcast(BF16)[0:8, 0:128], r_wsm[:].bitcast(BF16)[:, 0:8],
                    s_identb[:]),
                    deps=TOK([twm, t_w["identb"]], sl_pb[0].wdeps()))
                sl_pb[0].write(ttb)
                tevw = S.op("scalar", lambda h, ts=ts: h.activation(
                    wts[:, ts * 128:(ts + 1) * 128],
                    pb[0][:].bitcast(BF16)[0:8, 0:128], AF.Copy,
                    scale=1.0, bias=0.0),
                    deps=TOK([ttb], sl_wts.wdeps() if ts == 0 else []))
                sl_pb[0].read(tevw)
                if ts == 0:
                    sl_wts.write(tevw)
                prev_soft = {"tex": tex, "twm": twm, "ttb": ttb}
                t_wts = tevw
            if _B0_LIMIT < 10:
                continue
            twbc = S.op("tensor", lambda h: h.matmul(
                pb[0][:, 0:TC], s_rep[:], wts, start=True, stop=True),
                deps=TOK([t_w["rep"], t_wts], sl_pb[0].wdeps()))
            sl_pb[0].write(twbc)
            sl_wts.read(twbc)
            tevh = S.op("scalar", lambda h: h.activation(
                h_v, pb[6][:, 0:TC], AF.Copy, scale=1.0, bias=0.0),
                deps=TOK([t_hd], sl_hv.wdeps()))
            sl_pb[6].read(tevh)
            sl_hv.write(tevh)
            twh = S.op("vector", lambda h, cc=cc: h.tensor_tensor(
                wh_v[:, cc * TC:(cc + 1) * TC], h_v, pb[0][:, 0:TC], OP.mult),
                deps=TOK([tevh, twbc], sl_wh.wdeps() if cc == 0 else []))
            sl_pb[0].read(twh)
            sl_hv.read(twh)
            if cc == 0:
                sl_wh.write(twh)
            t_wh = twh
        t_wsl_tok = [None, None]
        for ob in range(2 if _B_LIMIT >= 3 else 0):
            t = S.dma("qg", f"wsl{ob % 2}", lambda h, ob=ob: h.dma_start(
                out=wsl[:, ob % 2], in_=d_wt[ob]),
                deps=TOK(sl_wsl[ob % 2].wdeps(),
                         allbar if (half == 0 and ob == 0) else []))
            sl_wsl[ob % 2].write(t)
            t_wsl_tok[ob % 2] = t
        for ob in range(NB if _B_LIMIT >= 3 else 0):
            pw = ob % 2
            t_res = [None] * 4
            for tcx in range(2):
                t_res[tcx] = S.dma("qs", f"res{tcx % 2}", lambda h, ob=ob, half=half, tcx=tcx: h.dma_start(
                    out=res_v[tcx % 2].bitcast(F32R),
                    in_=d_xmid[half * 4 + tcx, :, ob, :]),
                    deps=TOK(sl_res[tcx % 2].wdeps()))
                sl_res[tcx % 2].write(t_res[tcx])
            for tch in range(4 if _M_LIMIT >= 2 else 0):
                pk = tch % 2
                if tch >= 2:
                    t_res[tch] = S.dma("qs", f"res{tch % 2}", lambda h, ob=ob, half=half, tch=tch: h.dma_start(
                        out=res_v[tch % 2].bitcast(F32R),
                        in_=d_xmid[half * 4 + tch, :, ob, :]),
                        deps=TOK(sl_res[tch % 2].wdeps()))
                    sl_res[tch % 2].write(t_res[tch])
                t_last = None
                for k in range(NB):
                    t_last = S.op("tensor", lambda h, k=k, ob=ob, tch=tch, pk=pk: h.matmul(
                        pb[2 + pk][:, 0:TC], wsl[:, ob % 2, k],
                        xln2h[:, k, tch * TC:(tch + 1) * TC],
                        start=(k == 0), stop=False),
                        deps=TOK([t_wsl_tok[pw] if k == 0 else None,
                                  t_wh if k == 0 else None],
                                 sl_pb[2 + pk].wdeps() if k == 0 else []))
                    if k == 0:
                        sl_pb[2 + pk].write(t_last)
                    sl_wsl[ob % 2].read(t_last)
                    sl_xl2.read(t_last)
                if _M_LIMIT >= 3:
                    t_last = S.op("tensor", lambda h, ob=ob, tch=tch, pk=pk: h.matmul(
                        pb[2 + pk][:, 0:TC], bmv[:, ob],
                        wh_v[:, tch * TC:(tch + 1) * TC],
                        start=False, stop=True),
                        deps=TOK([t_bm, t_wh]))
                    sl_wh.read(t_last)
                else:
                    t_last = S.op("tensor", lambda h, ob=ob, tch=tch, pk=pk: h.matmul(
                        pb[2 + pk][:, 0:TC], s_ident[:],
                        prods[:, 0:TC], start=False, stop=True),
                        deps=TOK([t_w["ident"]]))
                if _M_LIMIT >= 4:
                    tev = S.op("vector", lambda h, ob=ob, tch=tch, pk=pk:
                               h.scalar_tensor_tensor(
                        outs_v[pk], pb[2 + pk][:, 0:TC], s_bcol[:, ob:ob + 1],
                        res_v[tch % 2].bitcast(F32R).bitcast(F32),
                        OP.add, OP.add),
                        deps=TOK([t_last, t_res[tch], t_w["bcol"]],
                                 sl_outs[pk].wdeps()))
                else:
                    tev = S.op("vector", lambda h, ob=ob, tch=tch, pk=pk:
                               h.tensor_copy(outs_v[pk], pb[2 + pk][:, 0:TC]),
                        deps=TOK([t_last, t_res[tch]], sl_outs[pk].wdeps()))
                sl_pb[2 + pk].read(tev)
                sl_res[tch % 2].read(tev)
                sl_outs[pk].write(tev)
                tso = S.dma("qs", f"st{pk}", lambda h, ob=ob, half=half, tch=tch, pk=pk:
                            h.dma_start(out=d_out[ob, half, tch], in_=outs_v[pk]),
                            deps=TOK([tev]))
                sl_outs[pk].read(tso)
            if ob + 2 < NB:
                t = S.dma("qg", f"wsl{pw}", lambda h, ob=ob: h.dma_start(
                    out=wsl[:, ob % 2], in_=d_wt[ob + 2]),
                    deps=TOK(sl_wsl[pw].wdeps()))
                sl_wsl[pw].write(t)
                t_wsl_tok[pw] = t


    emitter = S.finalize()
    with contextlib.ExitStack() as es2:
        for nm in list(ENGS) + sorted(S.qcnt):
            S.sems[nm] = es2.enter_context(nc.semaphore("sem_" + nm))
        with nc.Block() as block:
            block.sync(emitter("sync"))
            block.tensor(emitter("tensor"))
            block.vector(emitter("vector"))
            block.scalar(emitter("scalar"))
            block.gpsimd(emitter("gpsimd"))
    es.close()
    return nc


def _host_layout(inputs):
    import ml_dtypes
    x = np.asarray(inputs["L0_attn_x"], np.float32)
    xw = np.asarray(inputs["L0_attn_x_proj_w"], np.float32)
    dtw = np.asarray(inputs["L0_attn_dt_proj_w"], np.float32)
    alog = np.asarray(inputs["L0_attn_A_log"], np.float32)
    dsk = np.asarray(inputs["L0_attn_D"], np.float32)
    dtbv = np.asarray(inputs["L0_attn_dt_proj_b"], np.float32)
    W = np.asarray(inputs["L0_ffn_W"], np.float32)
    bb = np.asarray(inputs["L0_ffn_b"], np.float32)
    Wq = np.asarray(inputs["L0_ffn_Wq"], np.float32)
    Wk = np.asarray(inputs["L0_ffn_Wk"], np.float32)
    Aff = np.asarray(inputs["L0_ffn_A"], np.float32)
    Bm = np.asarray(inputs["L0_ffn_Bm"], np.float32)

    common = {}
    xw1 = xw[:128].T.reshape(NB, 128, 128).transpose(1, 0, 2)
    common["xw1"] = np.ascontiguousarray(
        xw1.reshape(128, 8, 4, 128).transpose(1, 0, 2, 3))
    common["xw2"] = np.ascontiguousarray(
        xw[128:160].T.reshape(NB, 128, 32).transpose(1, 0, 2).reshape(128, NB * 32))
    dtw1 = dtw.reshape(NB, 128, 128).transpose(2, 0, 1)
    common["dtw"] = np.ascontiguousarray(
        dtw1.reshape(128, 8, 4, 128).transpose(1, 0, 2, 3))
    common["alog"] = np.ascontiguousarray(
        alog.reshape(NB, 128, 16).transpose(1, 0, 2).reshape(128, NB * 16))
    common["dsb"] = np.ascontiguousarray(dsk.reshape(NB, 128).T)
    common["dtb"] = np.ascontiguousarray(dtbv.reshape(NB, 128).T)
    common["bcol"] = np.ascontiguousarray(bb.reshape(NB, 128).T)
    cst = np.zeros((128, 3), np.float32)
    cst[:, 0] = 1e-5
    cst[:, 1] = -0.5
    cst[:, 2] = 1.0 / np.sqrt(DK)
    common["cst"] = cst
    common["onesr"] = np.ones((65, 128), np.float32)
    common["onesb"] = np.ones((33, 128), ml_dtypes.bfloat16)
    common["onesc"] = np.ones((128, 1), np.float32)
    common["onesbc"] = np.ones((128, 1), ml_dtypes.bfloat16)
    common["ident"] = np.eye(128, dtype=np.float32)
    common["identb"] = np.eye(128, dtype=np.float32).astype(ml_dtypes.bfloat16)
    seg = np.zeros((128, 8), np.float32)
    for p in range(128):
        seg[p, p // 32] = 1.0
        seg[p, 4 + p // 32] = 1.0
    common["seg"] = seg.astype(ml_dtypes.bfloat16)
    rep = np.zeros((8, 128), np.float32)
    for c2 in range(128):
        rep[c2 // 16, c2] = 1.0
    common["rep"] = rep
    common["wq"] = np.ascontiguousarray(
        Wq.T.reshape(NB, 128, 256).transpose(1, 0, 2).reshape(128, NB * 256))
    common["wk"] = np.ascontiguousarray(
        Wk.T.reshape(NB, 128, 256).transpose(1, 0, 2).reshape(128, NB * 256))
    common["aff"] = np.ascontiguousarray(
        Aff.transpose(2, 0, 1).reshape(DM, 128).reshape(NB, 128, 128)
        .transpose(1, 0, 2).reshape(128, NB * 128))
    common["bm"] = np.ascontiguousarray(Bm.transpose(0, 2, 1).reshape(128, DM))
    common["wt"] = np.ascontiguousarray(
        W.T.reshape(NB, 128, NB, 128).transpose(2, 1, 0, 3))
    in_maps = []
    for b in range(8):
        m = dict(common)
        m["x"] = np.ascontiguousarray(
            x[b].T.reshape(NB, 128, NCH, TC).transpose(2, 1, 0, 3))
        in_maps.append(m)
    return in_maps


def kernel(**inputs):
    in_maps = _host_layout(inputs)
    nc = build_nc()
    res = run_bass_kernel_spmd(nc, in_maps, core_ids=list(range(8)))
    outs = []
    for b in range(8):
        o = np.asarray(res.results[b]["out"], np.float32)
        o = o.transpose(1, 2, 4, 0, 3).reshape(L, DM)
        outs.append(o)
    return np.stack(outs).astype(np.float32)
